# version 02
# baseline (speedup 1.0000x reference)
import contextlib

import numpy as np
import ml_dtypes

import concourse.bass as bass
import concourse.mybir as mybir
from concourse.bass_utils import run_bass_kernel_spmd
from concourse.alu_op_type import AluOpType as OP

F32 = mybir.dt.float32
BF16 = mybir.dt.bfloat16
AF = mybir.ActivationFunctionType
BF = ml_dtypes.bfloat16

T = 1024
TC = 8
D = 4096
KC = 32
NHC = 67
NXBC = 34
NH = 64
HD = 64
DH = 11008
NFH = 86
EPS = 1e-5
NCORES = 8

ENGINES = ("t", "v", "s", "g", "y")


class K:

    def __init__(self, nc):
        self.nc = nc
        self.prog = {e: [] for e in ENGINES}
        self.sems = {}
        self.stack = contextlib.ExitStack()
        self.guards = {}

    def sem(self, name):
        if name not in self.sems:
            h = self.stack.enter_context(self.nc.semaphore(name))
            self.sems[name] = [h, 0]
        return self.sems[name]

    def wait(self, eng, name, val=None):
        s = self.sem(name)
        v = s[1] if val is None else val
        if v <= 0:
            return
        h = s[0]
        self.prog[eng].append(lambda e, h=h, v=v: e.wait_ge(h, v))

    def emit(self, eng, fn, inc=None, n=1):
        if inc is None:
            self.prog[eng].append(fn)
            return None
        s = self.sem(inc)
        s[1] += n
        h = s[0]
        if eng == "t":
            self.prog[eng].append(fn)

            def thunk(e, h=h, n=n):
                e.drain().then_inc(h, n)
        else:
            def thunk(e, fn=fn, h=h, n=n):
                fn(e).then_inc(h, n)
        self.prog[eng].append(thunk)
        return s[1]

    def dma(self, eng, out, in_, inc="sd", **kw):
        return self.emit(
            eng,
            lambda e, out=out, in_=in_, kw=kw: e.dma_start(out=out, in_=in_, **kw),
            inc=inc, n=16)

    def cnt(self, name):
        return self.sem(name)[1]

    def use(self, eng, res):
        g = self.guards.get(res)
        if g:
            self.wait(eng, g[0], g[1])

    def guard(self, res, sem_name):
        self.guards[res] = (sem_name, self.cnt(sem_name))

    def chain(self, eng, fns, inc):
        for i, fn in enumerate(fns):
            if i > 0:
                self.wait(eng, inc)
            self.emit(eng, fn, inc=inc)

    def run(self):
        with self.nc.Block() as block:
            @block.tensor
            def _(e):
                for th in self.prog["t"]:
                    th(e)

            @block.vector
            def _(e):
                for th in self.prog["v"]:
                    th(e)

            @block.scalar
            def _(e):
                for th in self.prog["s"]:
                    th(e)

            @block.gpsimd
            def _(e):
                for th in self.prog["g"]:
                    th(e)

            @block.sync
            def _(e):
                for th in self.prog["y"]:
                    th(e)



def relayout(inputs):
    C = np.ascontiguousarray
    _ = inputs.get("L0_attn_residual")
    x = np.asarray(inputs["L0_attn_hidden_states"], np.float32).reshape(8192, D)
    wip = np.asarray(inputs["L0_attn_in_proj_w"], np.float32)
    wop = np.asarray(inputs["L0_attn_out_proj_w"], np.float32)
    convw = np.asarray(inputs["L0_attn_conv_w"], np.float32)
    alog = np.asarray(inputs["L0_attn_A_log"], np.float32)
    dtb = np.asarray(inputs["L0_attn_dt_bias"], np.float32)
    Dv = np.asarray(inputs["L0_attn_D"], np.float32)
    lnw = np.asarray(inputs["L0_attn_ln_w"], np.float32)
    norw = np.asarray(inputs["L0_attn_norm_w"], np.float32)
    mup = np.asarray(inputs["L0_attn_mup_vector"], np.float32)
    wg = np.asarray(inputs["L0_ffn_gate_w"], np.float32)
    wu = np.asarray(inputs["L0_ffn_up_w"], np.float32)
    wd = np.asarray(inputs["L0_ffn_down_w"], np.float32)
    ga = np.asarray(inputs["L0_ffn_gate_a"], np.float32)
    gb = np.asarray(inputs["L0_ffn_gate_b"], np.float32)
    ua = np.asarray(inputs["L0_ffn_up_a"], np.float32)
    ub = np.asarray(inputs["L0_ffn_up_b"], np.float32)
    da = np.asarray(inputs["L0_ffn_down_a"], np.float32)
    db = np.asarray(inputs["L0_ffn_down_b"], np.float32)

    def pad_cols(a, n):
        return np.concatenate([a, np.zeros((a.shape[0], n - a.shape[1]), a.dtype)], 1)

    shared = dict(
        wip=C(pad_cols(wip, NHC * 128).reshape(KC, 128, NHC, 128).transpose(2, 0, 1, 3)),
        wop=C(wop.reshape(KC, 128, 32, 128).transpose(2, 0, 1, 3)),
        wg=C(wg.T.reshape(KC, 128, NFH, 128).transpose(2, 0, 1, 3)),
        wu=C(wu.T.reshape(KC, 128, NFH, 128).transpose(2, 0, 1, 3)),
        wd=C(wd.T.reshape(NFH, 128, 32, 128).transpose(2, 0, 1, 3)),
        wga=C(np.concatenate([ga.T.reshape(KC, 128, 16),
                              np.zeros((KC, 128, 112), np.float32)], 2)),
        wua=C(np.concatenate([ua.T.reshape(KC, 128, 16),
                              np.zeros((KC, 128, 112), np.float32)], 2)),
        wgb=C(np.concatenate([gb.T.reshape(16, NFH, 128).transpose(1, 0, 2),
                              np.zeros((NFH, 112, 128), np.float32)], 1)),
        wub=C(np.concatenate([ub.T.reshape(16, NFH, 128).transpose(1, 0, 2),
                              np.zeros((NFH, 112, 128), np.float32)], 1)),
        wda=C(np.concatenate([da.T.reshape(NFH, 128, 16),
                              np.zeros((NFH, 128, 112), np.float32)], 2)),
        wdb=C(np.concatenate([db.T.reshape(16, 32, 128).transpose(1, 0, 2),
                              np.zeros((32, 112, 128), np.float32)], 1)),
        convw=C(convw.reshape(NXBC, 128, 4)),
        mup=C(pad_cols(mup[None, :], NHC * 128).reshape(NHC, 128).T),
        lnw=C(lnw.reshape(KC, 128).T),
        norw=C(norw.reshape(KC, 128).T),
        d_rep=C(np.broadcast_to(np.repeat(Dv, HD)[None, :], (128, 4096)).copy()),
        alog=C(alog[:, None]),
        dtb=C(dtb[:, None]),
        tri=np.triu(np.ones((128, 128), np.float32)).astype(BF),
        idn=np.eye(128, dtype=np.float32).astype(BF),
        idnf=np.eye(128, dtype=np.float32),
        ones1f=np.ones((1, 128), np.float32),
        bc128=np.concatenate([np.ones((1, 128), np.float32),
                              np.zeros((127, 128), np.float32)], 0),
        epsc=np.full((128, 1), EPS, np.float32),
    )

    per_core = []
    for c in range(NCORES):
        t0 = c * T
        halo = np.zeros((3, D), np.float32)
        if c % 4 != 0:
            halo = C(x[t0 - 3:t0])
        alpha = np.zeros((128, 128), np.float32)
        pmask = np.zeros((8, 1), np.float32)
        bbase, q = (c // 4) * 4, c % 4
        for j in range(q):
            pmask[bbase + j, 0] = 1.0
            for i in range(j + 1, q):
                alpha[bbase + i, bbase + j] = 1.0
        m = dict(shared)
        m.update(x=C(x[t0:t0 + T]), xh=halo, alpha=alpha, pmask=pmask)
        per_core.append(m)
    return per_core



class NS:
    def __init__(self, **kw):
        self.__dict__.update(kw)


def declare_dram(nc, launch, debug=()):
    io = NS()
    L = launch
    io._dbg = debug

    def inp(name, shape, ty=F32):
        setattr(io, name, nc.dram_tensor(name, shape, ty, kind="ExternalInput").ap())

    def scr(name, shape, ty, prod, cons):
        if L is None:
            kind = "ExternalOutput" if name in debug else "Internal"
            ap = nc.dram_tensor(name, shape, ty, kind=kind).ap()
        elif L == prod:
            ap = nc.dram_tensor(name, shape, ty, kind="ExternalOutput").ap()
        elif L in cons:
            ap = nc.dram_tensor(name, shape, ty, kind="ExternalInput").ap()
        else:
            ap = None
        setattr(io, name, ap)

    if L in (None, "L1", "L4"):
        inp("x", [T, D])
    if L in (None, "L1"):
        inp("xh", [3, D])
        inp("wip", [NHC, KC, 128, 128])
        inp("convw", [NXBC, 128, 4])
        inp("mup", [128, NHC])
        inp("lnw", [128, KC])
        inp("alog", [NH, 1])
        inp("dtb", [NH, 1])
        inp("d_rep", [128, D])
    if L in (None, "L2"):
        inp("wg", [NFH, KC, 128, 128])
        inp("wu", [NFH, KC, 128, 128])
        inp("wd", [32, NFH, 128, 128])
        inp("wga", [KC, 128, 128])
        inp("wua", [KC, 128, 128])
        inp("wgb", [NFH, 128, 128])
        inp("wub", [NFH, 128, 128])
        inp("wda", [NFH, 128, 128])
        inp("wdb", [32, 128, 128])
    if L in (None, "L3"):
        inp("wop", [32, KC, 128, 128])
        inp("norw", [128, KC])
        inp("alpha", [128, 128])
        inp("pmask", [8, 1])
    inp("tri", [128, 128], BF16)
    inp("idn", [128, 128], BF16)
    inp("idnf", [128, 128])
    inp("ones1f", [1, 128])
    inp("bc128", [128, 128])
    inp("epsc", [128, 1])

    scr("xn_sp", [KC, 128, T], BF16, "L1", ["L2"])
    scr("z_sp", [TC, 128, D], BF16, "L1", ["L3"])
    scr("y_sp", [TC, 128, D], BF16, "L1", ["L3"])
    scr("al_sp", [NH, T], F32, "L1", ["L3"])
    scr("cfm_sp", [128, T], BF16, "L1", ["L3"])
    if L is None:
        io.acc = nc.dram_tensor(
            "acc", [32, TC, 128, 128], F32,
            kind="ExternalOutput" if "acc" in debug else "Internal").ap()
        io.acc2 = io.acc
    else:
        scr("acc", [32, TC, 128, 128], F32, "L2", ["L3"])
        if L == "L3":
            io.acc2 = nc.dram_tensor("acc2", [32, TC, 128, 128], F32,
                                     kind="ExternalOutput").ap()
        elif L == "L4":
            io.acc2 = nc.dram_tensor("acc2", [32, TC, 128, 128], F32,
                                     kind="ExternalInput").ap()
    if L in (None, "L1"):
        io.cc_in = nc.dram_tensor("cc_in", [129, D], F32).ap()
    if L in (None, "L1"):
        io.ag_i = nc.dram_tensor("ag_i", [129 * NCORES, D], F32).ap()
    scr("ag", [129 * NCORES, D], F32, "L1", ["L3"])
    if L is None:
        io.ag = io.ag_i
    if L in (None, "L4"):
        io.out = nc.dram_tensor("out", [T, D], F32, kind="ExternalOutput").ap()
    io.zfm_dbg = None
    if "zfm" in debug:
        io.zfm_dbg = nc.dram_tensor("zfm_dbg", [32, 128, T], BF16,
                                    kind="ExternalOutput").ap()
    io.ps_dbg = None
    if "ps" in debug:
        io.ps_dbg = nc.dram_tensor("ps_dbg", [128, 1024], F32,
                                   kind="ExternalOutput").ap()
    return io


def alloc_bufs(nc, st):
    b = NS()
    E = st.enter_context
    b.AB1 = E(nc.sbuf_tensor("AB1", [128, 33024], BF16))
    b.AB2 = E(nc.sbuf_tensor("AB2", [128, 33024], BF16))
    b.AB3 = E(nc.sbuf_tensor("AB3", [128, 8192], BF16))
    b.H = E(nc.sbuf_tensor("Hbuf", [128, 4096], F32))
    b.F1 = E(nc.sbuf_tensor("F1", [128, 2624], F32))
    b.SM1 = E(nc.sbuf_tensor("SM1", [128, 3072], F32))
    b.SM2 = E(nc.sbuf_tensor("SM2", [128, 16], F32))
    b.SM3 = E(nc.sbuf_tensor("SM3", [128, 288], F32))
    b.SM4 = E(nc.sbuf_tensor("SM4", [128, 96], F32))
    b.BC = E(nc.sbuf_tensor("BCbuf", [128, 3072], BF16))
    b.LT = E(nc.sbuf_tensor("LTbuf", [128, 256], F32))
    b.TRI = E(nc.sbuf_tensor("TRIbuf", [128, 128], BF16))
    b.CST = E(nc.sbuf_tensor("CSTbuf", [128, 256], BF16))
    b.CSTF = E(nc.sbuf_tensor("CSTFbuf", [128, 288], F32))
    b.MUP = E(nc.sbuf_tensor("MUPbuf", [128, NHC + 1], F32))
    b.pb = [E(nc.psum_tensor(f"pb{i}", [128, 512], F32)) for i in range(8)]
    return b


def pbf(bank):
    return bank.ap().bitcast(BF16)



def load_consts(kk, io, b, launch):
    kk.dma("g", b.CST[:, 0:128], io.idn)
    kk.dma("g", b.CST[:, 128:256], io.tri)
    kk.dma("g", b.CSTF[:, 0:128], io.idnf)
    kk.dma("g", b.CSTF[:, 128:256], io.bc128)
    kk.dma("g", b.SM2[:, 8:9], io.epsc)
    if launch in (None, "L1"):
        kk.dma("g", b.CSTF[:, 256:256 + KC], io.lnw)
        kk.dma("g", b.MUP[:, 0:NHC], io.mup)
        kk.dma("g", b.SM3[:, 0:NXBC * 4],
               io.convw.transpose([1, 0, 2]))
        kk.dma("g", b.SM3[0:NH, 140:141], io.alog)
        kk.dma("g", b.SM3[0:NH, 142:143], io.dtb)
    return kk.cnt("sd")



def emit_ln(kk, io, b):
    AB1, AB2, SM = b.AB1, b.AB2, b.SM2
    ab2f = AB2.ap().bitcast(F32)
    xring = [ab2f[:, 0:4096], ab2f[:, 4096:8192]]
    xn_tm = AB2[:, 16384:20480]
    xnr_tm = AB2[:, 20480:24576]
    scr1 = AB2[:, 24576:28672]
    scr2 = AB2[:, 28672:32768]
    stg = [b.F1[:, 0:256].bitcast(BF16), b.F1[:, 256:512].bitcast(BF16)]
    idn = b.CST[:, 0:128]
    lnw = b.CSTF

    cons = kk.cnt("sd")
    for tc in range(TC + 1):
        haloT = tc == TC
        rows = 3 if haloT else 128
        r_ = slice(0, rows)
        src = io.xh if haloT else io.x.rearrange("(tc p) d -> tc p d", p=128)[tc]
        xt = xring[tc % 2][r_, :]
        kk.use("y", f"xring{tc % 2}")
        kk.dma("y", xt, src, inc="sdy")
        ld = kk.cnt("sdy")

        kk.wait("s", "sdy", ld)
        kk.use("s", "smstat")
        kk.chain("s", [
            lambda e, xt=xt, o=scr1[r_, :], a=SM[r_, 0:1]:
                e.activation(out=o, in_=xt, func=AF.Copy, accum_out=a),
            lambda e, xt=xt, o=scr2[r_, :], a=SM[r_, 1:2]:
                e.activation(out=o, in_=xt, func=AF.Square, accum_out=a),
        ], inc="ss")
        kk.wait("v", "ss", kk.cnt("ss"))
        kk.chain("v", [
            lambda e, r_=r_: e.tensor_scalar(
                out=SM[r_, 2:3], in0=SM[r_, 0:1], scalar1=1.0 / D,
                scalar2=None, op0=OP.mult),
            lambda e, r_=r_: e.tensor_tensor(
                out=SM[r_, 6:7], in0=SM[r_, 2:3], in1=SM[r_, 2:3], op=OP.mult),
            lambda e, r_=r_: e.scalar_tensor_tensor(
                out=SM[r_, 3:4], in0=SM[r_, 1:2], scalar=1.0 / D,
                in1=SM[r_, 6:7], op0=OP.mult, op1=OP.subtract),
        ], inc="sv")
        kk.guard("smstat", "sv")
        kk.wait("s", "sv", kk.cnt("sv"))
        kk.chain("s", [
            lambda e, r_=r_: e.activation(out=SM[r_, 6:7], in_=SM[r_, 3:4],
                                          func=AF.Ln, bias=SM[r_, 8:9]),
            lambda e, r_=r_: e.activation(out=SM[r_, 4:5], in_=SM[r_, 6:7],
                                          func=AF.Exp, scale=-0.5),
            lambda e, r_=r_: e.activation(out=SM[r_, 7:8], in_=SM[r_, 6:7],
                                          func=AF.Exp, scale=-1.0),
        ], inc="ss")
        kk.wait("v", "ss", kk.cnt("ss"))
        kk.chain("v", [
            lambda e, r_=r_: e.scalar_tensor_tensor(
                out=SM[r_, 6:7], in0=SM[r_, 3:4], scalar=0.0,
                in1=SM[r_, 7:8], op0=OP.add, op1=OP.mult),
            lambda e, r_=r_: e.tensor_scalar(
                out=SM[r_, 6:7], in0=SM[r_, 6:7], scalar1=EPS,
                scalar2=None, op0=OP.add),
        ], inc="sv")
        kk.wait("s", "sv", kk.cnt("sv"))
        kk.chain("s", [
            lambda e, r_=r_: e.activation(out=SM[r_, 5:6], in_=SM[r_, 6:7],
                                          func=AF.Ln),
            lambda e, r_=r_: e.activation(out=SM[r_, 5:6], in_=SM[r_, 5:6],
                                          func=AF.Exp, scale=-0.5),
        ], inc="ss")
        kk.wait("v", "ss", kk.cnt("ss"))
        kk.use("v", "xtmbufs")
        kk.chain("v", [
            lambda e, r_=r_: e.tensor_tensor(out=SM[r_, 5:6], in0=SM[r_, 4:5],
                                             in1=SM[r_, 5:6], op=OP.mult),
            lambda e, xt=xt, r_=r_: e.tensor_scalar(
                out=xn_tm[r_, :], in0=xt, scalar1=SM[r_, 2:3],
                scalar2=SM[r_, 4:5], op0=OP.subtract, op1=OP.mult),
            lambda e, xt=xt, r_=r_: e.tensor_scalar(
                out=xnr_tm[r_, :], in0=xt, scalar1=SM[r_, 2:3],
                scalar2=SM[r_, 5:6], op0=OP.subtract, op1=OP.mult),
        ], inc="sv")
        tsd = kk.cnt("sv")
        kk.guard(f"xring{tc % 2}", "sv")

        kk.wait("t", "sv", tsd)
        kk.wait("t", "sd", cons)
        if haloT:
            for dc in range(KC):
                bi = 4 + (dc % 2)
                kk.use("t", f"bank{bi}")
                kk.emit("t", lambda e, i=xnr_tm[0:3, dc * 128:(dc + 1) * 128],
                        pv=pbf(b.pb[bi]), idn=idn:
                        e.transpose(out=pv[:, 0:3], in_=i, identity=idn[0:3, 0:3]),
                        inc="st")
                kk.wait("s", "st", kk.cnt("st"))
                kk.emit("s", lambda e, pv=pbf(b.pb[bi]), dc=dc:
                        e.activation(out=AB1[:, 32768 + dc * 8:32768 + dc * 8 + 3],
                                     in_=pv[:, 0:3], func=AF.Copy,
                                     scale=lnw[:, 256 + dc:257 + dc]), inc="ss")
                kk.guard(f"bank{bi}", "ss")
        else:
            for half in range(2):
                tm = xn_tm if half == 0 else xnr_tm
                for dq in range(8):
                    bi = (6 if half == 0 else 4) + (dq % 2)
                    kk.use("t", f"bank{bi}")
                    for j in range(4):
                        dc = dq * 4 + j
                        kk.emit("t", lambda e, i=tm[:, dc * 128:(dc + 1) * 128],
                                pv=pbf(b.pb[bi]), j=j, idn=idn:
                                e.transpose(out=pv[:, j * 128:(j + 1) * 128], in_=i,
                                            identity=idn),
                                inc=None if j < 3 else "st")
                    tdn = kk.cnt("st")
                    if half == 0:
                        sg = stg[dq % 2]
                        kk.wait("v", "st", tdn)
                        kk.use("v", f"stg{dq % 2}")
                        kk.emit("v", lambda e, pv=pbf(b.pb[bi]), sg=sg:
                                e.tensor_copy(out=sg, in_=pv[:, 0:512]), inc="sv")
                        kk.guard(f"bank{bi}", "sv")
                        kk.wait("y", "sv", kk.cnt("sv"))
                        dst = io.xn_sp[dq * 4:dq * 4 + 4, :,
                                       tc * 128:(tc + 1) * 128] \
                            .transpose([1, 0, 2])
                        kk.dma("y", dst, sg, inc="sdy")
                        kk.guard(f"stg{dq % 2}", "sdy")
                    else:
                        kk.wait("s", "st", tdn)
                        for j in range(4):
                            dc = dq * 4 + j
                            kk.emit("s", lambda e, pv=pbf(b.pb[bi]), j=j, dc=dc,
                                    tc=tc:
                                    e.activation(
                                        out=AB1[:, dc * 1024 + tc * 128:
                                                dc * 1024 + (tc + 1) * 128],
                                        in_=pv[:, j * 128:(j + 1) * 128],
                                        func=AF.Copy,
                                        scale=lnw[:, 256 + dc:257 + dc]),
                                    inc="ss")
                        kk.guard(f"bank{bi}", "ss")
            kk.guard("xtmbufs", "st")



def emit_inproj(kk, io, b):
    AB1, AB2, AB3 = b.AB1, b.AB2, b.AB3
    idn = b.CST[:, 0:128]
    wring = [AB3[:, 0:4096], AB3[:, 4096:8192]]
    xs_tm = AB2.ap()[:, 0:32768].rearrange("p (tc c) -> p tc c", c=4096)

    kk.wait("t", "ss", None)
    for hc in range(NHC):
        wid = 64 if hc == NHC - 1 else 128
        slot = wring[hc % 2]
        kk.use("g", f"wring{hc % 2}")
        kk.dma("g", slot, io.wip[hc].transpose([1, 0, 2]))
        wld = kk.cnt("sd")
        kk.wait("t", "sd", wld)

        is_xbc = 32 <= hc < 66
        segs = [(0, 512, (hc % 2) * 2 + 0), (512, 1024, (hc % 2) * 2 + 1)]
        if is_xbc:
            segs.append((-3, 0, 4 + hc % 2))
        for (c0, c1, bi) in segs:
            bank = b.pb[bi]
            kk.use("t", f"bank{bi}")
            for kc in range(KC):
                if c0 < 0:
                    rhs = AB1[:, 32768 + kc * 8:32768 + kc * 8 + 3]
                    out = bank[0:wid, 0:3]
                else:
                    rhs = AB1[:, kc * 1024 + c0:kc * 1024 + c1]
                    out = bank[0:wid, 0:(c1 - c0)]
                kk.emit("t", lambda e, out=out, w=slot[:, kc * 128:kc * 128 + wid],
                        rhs=rhs, st=(kc == 0), sp=(kc == KC - 1):
                        e.matmul(out=out, lhsT=w, rhs=rhs, start=st, stop=sp),
                        inc=None if kc < KC - 1 else "st")
        kk.guard(f"wring{hc % 2}", "st")
        mmdone = kk.cnt("st")
        mupc = b.MUP[:, hc:hc + 1]

        if hc < 32:
            zfm = b.BC[:, 2048:3072]
            kk.wait("v", "st", mmdone)
            if io.ps_dbg is not None and hc == 0:
                for (c0, c1, bi) in segs:
                    kk.emit("v", lambda e, bi=bi, c0=c0, c1=c1: e.tensor_copy(
                        out=b.H[:, c0:c1], in_=b.pb[bi][:, 0:512]), inc="sv")
                kk.wait("y", "sv", kk.cnt("sv"))
                kk.dma("y", io.ps_dbg, b.H.ap()[:, 0:1024], inc="sdy")
            kk.use("v", "zpre")
            for (c0, c1, bi) in segs:
                kk.emit("v", lambda e, bi=bi, c0=c0, c1=c1, mupc=mupc:
                        e.tensor_scalar(
                            out=b.F1[:, 1056 + c0:1056 + c1],
                            in0=b.pb[bi][:, 0:512],
                            scalar1=mupc, scalar2=None, op0=OP.mult), inc="sv")
                kk.guard(f"bank{bi}", "sv")
            kk.wait("s", "sv", kk.cnt("sv"))
            kk.use("s", "zfm")
            kk.emit("s", lambda e: e.activation(
                out=zfm, in_=b.F1[:, 1056:2080], func=AF.Silu), inc="ss")
            kk.guard("zpre", "ss")
            zdone = kk.cnt("ss")
            if io.zfm_dbg is not None:
                kk.wait("y", "ss", zdone)
                kk.dma("y", io.zfm_dbg[hc], zfm, inc="sdy")
                kk.guard("zfm", "sdy")
            kk.wait("t", "ss", zdone)
            for g in range(2):
                bi = 6 + g
                kk.use("t", f"bank{bi}")
                for j in range(4):
                    tcb = g * 4 + j
                    kk.emit("t", lambda e, i=zfm[:, tcb * 128:(tcb + 1) * 128],
                            pv=pbf(b.pb[bi]), j=j, idn=idn:
                            e.transpose(out=pv[:, j * 128:(j + 1) * 128], in_=i,
                                        identity=idn), inc=None if j < 3 else "st")
                kk.wait("v", "st", kk.cnt("st"))
                kk.use("v", f"zstg{g}")
                sg = b.F1[:, 512 + g * 256:768 + g * 256].bitcast(BF16)
                kk.emit("v", lambda e, pv=pbf(b.pb[bi]), sg=sg:
                        e.tensor_copy(out=sg, in_=pv[:, 0:512]), inc="sv")
                kk.guard(f"bank{bi}", "sv")
                kk.wait("y", "sv", kk.cnt("sv"))
                dst = io.z_sp[g * 4:(g + 1) * 4, :, hc * 128:(hc + 1) * 128] \
                    .transpose([1, 0, 2])
                kk.dma("y", dst, sg, inc="sdy")
                kk.guard(f"zstg{g}", "sdy")
            kk.guard("zfm", "st")
        elif hc < 66:
            xc = hc - 32
            ping = b.F1[:, 1056 * (hc % 2):1056 * (hc % 2) + 1056]
            kk.wait("s", "st", mmdone)
            kk.use("s", f"convp{hc % 2}")
            for (c0, c1, bi) in segs:
                if c0 < 0:
                    o, i = ping[:, 0:3], b.pb[bi][:, 0:3]
                else:
                    o, i = ping[:, 3 + c0:3 + c1], b.pb[bi][:, 0:512]
                kk.emit("s", lambda e, o=o, i=i, mupc=mupc:
                        e.activation(out=o, in_=i, func=AF.Copy, scale=mupc),
                        inc="ss")
                kk.guard(f"bank{bi}", "ss")
            evd = kk.cnt("ss")
            cacc = b.SM1[:, 2048:3072]
            kk.wait("v", "ss", evd)
            kk.use("v", "cacc")

            cfns = [lambda e, ping=ping, xc=xc, cacc=cacc: e.tensor_scalar(
                out=cacc, in0=ping[:, 0:1024],
                scalar1=b.SM3[:, xc * 4:xc * 4 + 1], scalar2=None, op0=OP.mult)]
            for k2 in range(1, 4):
                cfns.append(lambda e, ping=ping, xc=xc, cacc=cacc, k2=k2:
                            e.scalar_tensor_tensor(
                                out=cacc, in0=ping[:, k2:k2 + 1024],
                                scalar=b.SM3[:, xc * 4 + k2:xc * 4 + k2 + 1],
                                in1=cacc, op0=OP.mult, op1=OP.add))
            kk.chain("v", cfns, inc="sv")
            cvd = kk.cnt("sv")
            kk.guard(f"convp{hc % 2}", "sv")
            kk.wait("s", "sv", cvd)
            if xc < 32:
                xbfm = b.F1[:, 2112:2624].bitcast(BF16)
                kk.use("s", "xbfm")
                kk.emit("s", lambda e, xbfm=xbfm, cacc=cacc:
                        e.activation(out=xbfm, in_=cacc, func=AF.Silu), inc="ss")
                kk.guard("cacc", "ss")
                kk.wait("t", "ss", kk.cnt("ss"))
                for g in range(2):
                    bi = 6 + g
                    kk.use("t", f"bank{bi}")
                    for j in range(4):
                        tcb = g * 4 + j
                        kk.emit("t", lambda e, i=xbfm[:, tcb * 128:(tcb + 1) * 128],
                                pv=pbf(b.pb[bi]), j=j, idn=idn:
                                e.transpose(out=pv[:, j * 128:(j + 1) * 128], in_=i,
                                            identity=idn),
                                inc=None if j < 3 else "st")
                    kk.wait("v", "st", kk.cnt("st"))
                    kk.emit("v", lambda e, pv=pbf(b.pb[bi]), g=g, xc=xc:
                            e.tensor_copy(
                                out=xs_tm[:, g * 4:(g + 1) * 4,
                                          xc * 128:(xc + 1) * 128],
                                in_=pv[:, 0:512].rearrange("p (j c) -> p j c",
                                                           c=128)),
                            inc="sv")
                    kk.guard(f"bank{bi}", "sv")
                kk.guard("xbfm", "st")
            else:
                dst = b.BC[:, 0:1024] if xc == 32 else b.BC[:, 1024:2048]
                kk.emit("s", lambda e, dst=dst, cacc=cacc:
                        e.activation(out=dst, in_=cacc, func=AF.Silu), inc="ss")
                kk.guard("cacc", "ss")
                sdn = kk.cnt("ss")
                if xc == 32:
                    kk.wait("t", "ss", sdn)
                    for g in range(2):
                        bi = 6 + g
                        kk.use("t", f"bank{bi}")
                        for j in range(4):
                            tcb = g * 4 + j
                            kk.emit("t", lambda e,
                                    i=dst[:, tcb * 128:(tcb + 1) * 128],
                                    pv=pbf(b.pb[bi]), j=j, idn=idn:
                                    e.transpose(out=pv[:, j * 128:(j + 1) * 128],
                                                in_=i, identity=idn),
                                    inc=None if j < 3 else "st")
                        kk.wait("v", "st", kk.cnt("st"))
                        kk.emit("v", lambda e, pv=pbf(b.pb[bi]), g=g:
                                e.tensor_copy(out=b.BC[:, 2048 + g * 512:
                                                       2048 + (g + 1) * 512],
                                              in_=pv[:, 0:512]), inc="sv")
                        kk.guard(f"bank{bi}", "sv")
                else:
                    kk.wait("y", "ss", sdn)
                    kk.dma("y", io.cfm_sp, dst, inc="sdy")
        else:
            kk.wait("s", "st", mmdone)
            for (c0, c1, bi) in segs:
                kk.emit("s", lambda e, bi=bi, c0=c0, c1=c1, mupc=mupc:
                        e.activation(out=b.SM1[0:64, c0:c1],
                                     in_=b.pb[bi][0:64, 0:512],
                                     func=AF.Copy, scale=mupc[0:64, :]), inc="ss")
                kk.guard(f"bank{bi}", "ss")



def emit_dtpipe(kk, io, b):
    SM1, SM4, SM3 = b.SM1, b.SM4, b.SM3
    dtr = SM1[0:64, 0:1024]
    tmp = SM1[0:64, 1024:2048]
    dtsp = SM1[0:64, 2048:3072]
    lp = [SM1[0:64, 0:1024], SM1[0:64, 1024:2048]]
    dtbc = SM3[0:64, 142:143]

    kk.wait("s", "ss", None)
    kk.chain("s", [
        lambda e: e.activation(out=SM3[0:64, 141:142], in_=SM3[0:64, 140:141],
                               func=AF.Exp),
        lambda e: e.activation(out=tmp, in_=dtr, func=AF.Abs, bias=dtbc),
        lambda e: e.activation(out=tmp, in_=tmp, func=AF.Exp, scale=-1.0),
        lambda e: e.activation(out=tmp, in_=tmp, func=AF.Ln, bias=1.0),
    ], inc="ss")
    sdn = kk.cnt("ss")
    kk.wait("v", "ss", sdn)
    kk.chain("v", [
        lambda e: e.tensor_scalar(out=SM3[0:64, 141:142], in0=SM3[0:64, 141:142],
                                  scalar1=-1.0, scalar2=None, op0=OP.mult),
        lambda e: e.tensor_scalar(out=dtsp, in0=dtr, scalar1=dtbc, scalar2=0.0,
                                  op0=OP.add, op1=OP.max),
        lambda e: e.tensor_tensor(out=dtsp, in0=dtsp, in1=tmp, op=OP.add),
        lambda e: e.tensor_scalar(out=lp[1], in0=dtsp,
                                  scalar1=SM3[0:64, 141:142], scalar2=None,
                                  op0=OP.mult),
    ], inc="sv")

    cur = 1
    for rnd in range(7):
        k = 1 << rnd
        src_, dst = lp[cur], lp[1 - cur]

        def round_(e, src_=src_, dst=dst, k=k):
            s3 = src_.rearrange("p (c t) -> p c t", t=128)
            d3 = dst.rearrange("p (c t) -> p c t", t=128)
            e.tensor_copy(out=d3[:, :, 0:k], in_=s3[:, :, 0:k])
            return e.tensor_tensor(out=d3[:, :, k:128], in0=s3[:, :, k:128],
                                   in1=s3[:, :, 0:128 - k], op=OP.add)
        kk.wait("v", "sv")
        kk.emit("v", round_, inc="sv")
        cur = 1 - cur
    lfin = lp[cur]

    kk.wait("v", "sv")
    kk.emit("v", lambda e: e.tensor_copy(
        out=SM4[0:64, 0:8],
        in_=lfin.rearrange("p (c t) -> p c t", t=128)[:, :, 127:128]
        .rearrange("p c t -> p (c t)")), inc="sv")
    kk.emit("v", lambda e: e.memset(SM4[0:64, 8:16], 0.0), inc="sv")
    ofns = []
    for c in range(1, 8):
        ofns.append(lambda e, c=c: e.tensor_tensor(
            out=SM4[0:64, 8 + c:9 + c], in0=SM4[0:64, 7 + c:8 + c],
            in1=SM4[0:64, c - 1:c], op=OP.add))
    ofns.append(lambda e: e.tensor_tensor(
        out=SM4[0:64, 16:17], in0=SM4[0:64, 15:16], in1=SM4[0:64, 7:8],
        op=OP.add))
    aloc = SM1[0:64, 1024:2048]

    def aloc_all(e):
        last = None
        for c in range(8):
            last = e.tensor_scalar(out=aloc[:, c * 128:(c + 1) * 128],
                                   in0=lfin[:, c * 128:(c + 1) * 128],
                                   scalar1=SM4[0:64, 8 + c:9 + c], scalar2=None,
                                   op0=OP.add)
        return last
    ofns.append(aloc_all)
    kk.wait("v", "sv")
    kk.chain("v", ofns, inc="sv")
    vdn = kk.cnt("sv")
    kk.wait("s", "sv", vdn)
    kk.emit("s", lambda e: e.activation(out=SM4[0:64, 24:32], in_=SM4[0:64, 0:8],
                                        func=AF.Exp), inc="ss")
    kk.wait("y", "sv", vdn)
    kk.dma("y", io.al_sp, SM1[0:64, 1024:2048], inc="sdy")
    return NS(aloc=SM1[0:64, 1024:2048], dtsp=dtsp, lfin=lfin)



def emit_pass1(kk, io, b, dd, p1max=99):
    AB1, AB2 = b.AB1, b.AB2
    idnf = b.CSTF[:, 0:128]
    ones1f = b.CSTF[0:1, 128:256]
    tri = b.CST[:, 128:256]
    H = b.H
    xs_tm = AB2.ap()[:, 0:32768].rearrange("p (tc c) -> p tc c", c=4096)
    Xdt = AB1[:, 0:4096]
    Db = AB1[:, 4096:8192]
    Hbf = AB1[:, 8192:12288]
    Mg = [AB1[:, 12288:13312], AB1[:, 13312:14336]]
    Eg = AB1[:, 14336:14848]
    ych = AB1[:, 14848:18944]
    arow = AB1.ap()[0:1, 18944:27136].bitcast(F32)
    xsD = AB1[:, 27136:31232]
    dt_tm = b.LT[:, 0:64]
    dece_tm = b.LT[:, 64:128]
    decin_tm = b.LT[:, 128:192]
    na_tm = b.LT[:, 192:256]
    dece_f = b.F1[0:64, 0:128]
    decin_f = b.F1[0:64, 128:256]
    na_f = b.F1[0:64, 256:384]
    Dm = b.F1[:, 512:1024]
    u = b.F1[:, 1024:1536]

    ipdone = kk.cnt("st")
    for eng in ("v", "s", "g", "y"):
        kk.wait(eng, "st", ipdone)
    if p1max >= -1:
        kk.dma("g", Db, io.d_rep)
    dbld = kk.cnt("sd")
    kk.wait("v", "sv", None)

    kk.chain("v", [lambda e: e.memset(H.ap(), 0.0),
                   lambda e: e.memset(Hbf, 0.0)], inc="sv")
    kk.wait("v", "sv", kk.cnt("sv"))

    for c in range(TC):
        if p1max < 1:
            continue
        ccol = slice(c * 128, (c + 1) * 128)
        kk.wait("s", "sv", None)
        kk.wait("s", "ss", None)
        kk.use("s", "decef")

        kk.emit("s", lambda e, c=c: e.activation(
            out=dece_f, in_=dd.lfin[:, c * 128:(c + 1) * 128],
            func=AF.Exp, scale=-1.0, bias=b.SM4[0:64, c:c + 1]), inc="ss")
        kk.emit("s", lambda e, c=c: e.activation(
            out=decin_f, in_=dd.lfin[:, c * 128:(c + 1) * 128], func=AF.Exp),
            inc="ss")
        sfmd = kk.cnt("ss")
        kk.wait("v", "ss", sfmd)
        kk.use("v", "naf")
        kk.emit("v", lambda e, c=c: e.tensor_scalar(
            out=na_f, in0=dd.aloc[:, c * 128:(c + 1) * 128], scalar1=-1.0,
            scalar2=None, op0=OP.mult), inc="sv")
        fmdone = kk.cnt("sv")

        kk.wait("t", "sv", fmdone)
        kk.wait("t", "ss", sfmd)
        for (src, dst, tag) in ((dd.dtsp[:, ccol], dt_tm, "dt"),
                                (dece_f, dece_tm, "de"),
                                (decin_f, decin_tm, "di"), (na_f, na_tm, "na")):
            kk.use("t", "bank7")
            kk.emit("t", lambda e, src=src: e.transpose(
                out=b.pb[7][:, 0:64], in_=src, identity=idnf[0:64, 0:64]),
                inc="st")
            kk.wait("v", "st", kk.cnt("st"))
            kk.use("v", "lt" + tag)
            kk.emit("v", lambda e, dst=dst: e.tensor_copy(
                out=dst, in_=b.pb[7][:, 0:64]), inc="sv")
            kk.guard("bank7", "sv")
            kk.guard("lt_src_" + tag, "st")
        kk.guard("decef", "st")
        kk.guard("naf", "st")
        tmv = kk.cnt("sv")

        if p1max < 2:
            continue
        kk.use("y", "arow")
        kk.dma("y", arow[0:1, 0:4096].rearrange("o (r t) -> o r t", t=128),
               dd.aloc[0:32, ccol], inc="sdy")
        arow_h = [kk.cnt("sdy"), None]

        kk.wait("g", "sv", tmv)
        kk.wait("g", "sd", dbld)
        kk.use("g", "Xdt")
        kk.emit("g", lambda e, c=c: e.tensor_tensor(
            out=Xdt.rearrange("p (r hd) -> p r hd", hd=64),
            in0=xs_tm[:, c, :].rearrange("p (r hd) -> p r hd", hd=64),
            in1=dt_tm.unsqueeze(2).broadcast_to([128, 64, 64]), op=OP.mult),
            inc="sg")
        xdt_done = kk.cnt("sg")
        kk.guard("ltdt", "sg")
        kk.use("g", "xsD")
        kk.emit("g", lambda e, c=c: e.tensor_tensor(
            out=xsD, in0=xs_tm[:, c, :], in1=Db, op=OP.mult), inc="sg")
        xsd_done = kk.cnt("sg")

        if p1max < 3:
            continue
        kk.use("t", "bank6")
        kk.emit("t", lambda e, c=c: e.matmul(
            out=b.pb[6][:, 0:128], lhsT=b.BC[:, c * 128:(c + 1) * 128],
            rhs=b.BC[:, 1024 + c * 128:1024 + (c + 1) * 128], start=True,
            stop=True), inc="st")
        kk.wait("v", "st", kk.cnt("st"))
        kk.use("v", "cbt")
        kk.emit("v", lambda e: e.tensor_tensor(out=b.TRI[:, 0:128],
                                               in0=b.pb[6][:, 0:128], in1=tri,
                                               op=OP.mult), inc="sv")
        kk.guard("bank6", "sv")
        cbt_done = kk.cnt("sv")

        if p1max < 4:
            continue
        for g8 in range(8):
            mslot = Mg[g8 % 2]
            for g4i in range(2):
                g4 = g8 * 2 + g4i
                if g4 == 8:
                    kk.wait("y", "st", kk.cnt("st"))
                    kk.dma("y", arow[0:1, 0:4096]
                           .rearrange("o (r t) -> o r t", t=128),
                           dd.aloc[32:64, ccol], inc="sdy")
                    arow_h[1] = kk.cnt("sdy")
                bi = g4 % 2
                kk.use("t", f"bank{bi}")
                kk.wait("t", "sdy", arow_h[0] if g4 < 8 else arow_h[1])
                kk.emit("t", lambda e, g4=g4, bi=bi: e.matmul(
                    out=b.pb[bi].ap(), lhsT=ones1f,
                    rhs=arow[0:1, (g4 % 8) * 512:(g4 % 8 + 1) * 512],
                    start=True, stop=True), inc="st")
                abd = kk.cnt("st")
                kk.wait("v", "st", abd)
                kk.wait("v", "sv", tmv)
                kk.use("v", "Dm")
                for hh in range(4):
                    r = g4 * 4 + hh
                    kk.emit("v", lambda e, bi=bi, hh=hh, r=r: e.tensor_scalar(
                        out=Dm[:, hh * 128:(hh + 1) * 128],
                        in0=b.pb[bi][:, hh * 128:(hh + 1) * 128],
                        scalar1=na_tm[:, r:r + 1], scalar2=87.0,
                        op0=OP.add, op1=OP.min), inc="sv")
                kk.guard(f"bank{bi}", "sv")
                dmd = kk.cnt("sv")
                kk.wait("s", "sv", dmd)
                kk.use("s", "Eg")
                kk.emit("s", lambda e: e.activation(out=Eg, in_=Dm, func=AF.Exp),
                        inc="ss")
                kk.guard("Dm", "ss")
                ed = kk.cnt("ss")
                kk.wait("v", "ss", ed)
                kk.wait("v", "sv", cbt_done)
                kk.use("v", f"mslot{g8 % 2}")
                kk.emit("v", lambda e, mslot=mslot, g4i=g4i: e.tensor_tensor(
                    out=mslot[:, g4i * 512:(g4i + 1) * 512]
                    .rearrange("p (h c) -> p h c", c=128),
                    in0=Eg.rearrange("p (h c) -> p h c", c=128),
                    in1=b.TRI[:, 0:128].unsqueeze(1).broadcast_to([128, 4, 128]),
                    op=OP.mult), inc="sv")
                kk.guard("Eg", "sv")
            mdone = kk.cnt("sv")

            if p1max < 5:
                continue
            bi_a = 2 + g8 % 2
            bi_i = 4 + g8 % 2
            kk.wait("t", "sv", mdone)
            kk.wait("t", "sg", xdt_done)
            kk.use("t", f"bank{bi_a}")
            kk.use("t", f"bank{bi_i}")
            kk.emit("t", lambda e, bi_i=bi_i, c=c, g8=g8: e.matmul(
                out=b.pb[bi_i].ap(),
                lhsT=b.BC[:, 1024 + c * 128:1024 + (c + 1) * 128],
                rhs=Hbf[:, g8 * 512:(g8 + 1) * 512], start=True, stop=True),
                inc="st")
            for hh in range(8):
                r = g8 * 8 + hh
                kk.emit("t", lambda e, hh=hh, r=r, mslot=mslot, bi_a=bi_a:
                        e.matmul(out=b.pb[bi_a][:, hh * 64:(hh + 1) * 64],
                                 lhsT=mslot[:, hh * 128:(hh + 1) * 128],
                                 rhs=Xdt[:, r * 64:(r + 1) * 64],
                                 start=(hh == 0), stop=(hh == 7)),
                        inc=None if hh < 7 else "st")
            kk.guard(f"mslot{g8 % 2}", "st")
            ydone = kk.cnt("st")

            kk.wait("v", "st", ydone)
            kk.wait("v", "sg", xsd_done)
            kk.use("v", "u")
            kk.use("v", "ych")

            yfns = [
                lambda e, bi_i=bi_i, g8=g8: e.tensor_tensor(
                    out=u.rearrange("p (r hd) -> p r hd", hd=64),
                    in0=b.pb[bi_i].ap().rearrange("p (r hd) -> p r hd", hd=64),
                    in1=decin_tm[:, g8 * 8:(g8 + 1) * 8].unsqueeze(2)
                    .broadcast_to([128, 8, 64]), op=OP.mult),
                lambda e, bi_a=bi_a: e.scalar_tensor_tensor(
                    out=u, in0=b.pb[bi_a].ap(), scalar=0.0, in1=u, op0=OP.add,
                    op1=OP.add),
                lambda e, g8=g8: e.tensor_tensor(
                    out=ych[:, g8 * 512:(g8 + 1) * 512], in0=u,
                    in1=xsD[:, g8 * 512:(g8 + 1) * 512], op=OP.add),
            ]
            kk.chain("v", yfns, inc="sv")
            kk.guard(f"bank{bi_a}", "sv")
            kk.guard(f"bank{bi_i}", "sv")
            kk.guard("lt" + "di", "sv")
        yevd = kk.cnt("sv")
        kk.wait("y", "sv", yevd)
        kk.dma("y", io.y_sp[c], ych, inc="sdy")
        kk.guard("ych", "sdy")
        kk.guard("xsD", "sdy")

        if p1max < 6:
            continue
        kk.wait("g", "st", kk.cnt("st"))
        kk.emit("g", lambda e: e.tensor_tensor(
            out=Xdt.rearrange("p (r hd) -> p r hd", hd=64),
            in0=Xdt.rearrange("p (r hd) -> p r hd", hd=64),
            in1=dece_tm.unsqueeze(2).broadcast_to([128, 64, 64]), op=OP.mult),
            inc="sg")
        kk.guard("ltde", "sg")
        xdtd = kk.cnt("sg")

        kk.use("y", "sm4row")
        if c == 0:
            kk.wait("y", "ss", None)
        kk.dma("y", b.SM4[0:1, 32:96], b.SM4[0:64, 24 + c:25 + c], inc="sdy")
        kk.wait("g", "sdy", kk.cnt("sdy"))
        kk.wait("g", "st", kk.cnt("st"))
        kk.emit("g", lambda e: e.tensor_copy(
            out=arow[0:1, 0:4096].rearrange("o (r j) -> o r j", j=64),
            in_=b.SM4[0:1, 32:96].unsqueeze(2).broadcast_to([1, 64, 64])),
            inc="sg")
        kk.guard("sm4row", "sg")
        dtrd = kk.cnt("sg")

        if p1max < 7:
            continue
        for seg in range(8):
            bi_s = 4 + seg % 2
            bi_d = seg % 2
            kk.wait("t", "sg", xdtd)
            kk.use("t", f"bank{bi_s}")
            kk.emit("t", lambda e, seg=seg, c=c: e.matmul(
                out=b.pb[4 + seg % 2].ap(),
                lhsT=b.BC[:, 2048 + c * 128:2048 + (c + 1) * 128],
                rhs=Xdt[:, seg * 512:(seg + 1) * 512], start=True, stop=True),
                inc="st")
            kk.wait("t", "sg", dtrd)
            kk.use("t", f"bank{bi_d}")
            kk.emit("t", lambda e, seg=seg: e.matmul(
                out=b.pb[seg % 2].ap(), lhsT=ones1f,
                rhs=arow[0:1, seg * 512:(seg + 1) * 512], start=True, stop=True),
                inc="st")
            smm = kk.cnt("st")
            kk.wait("v", "st", smm)
            hseg = H[:, seg * 512:(seg + 1) * 512]

            hfns = [
                lambda e, hseg=hseg, bi_d=bi_d: e.tensor_tensor(
                    out=hseg, in0=hseg, in1=b.pb[bi_d].ap(), op=OP.mult),
                lambda e, hseg=hseg, bi_s=bi_s: e.tensor_tensor(
                    out=hseg, in0=hseg, in1=b.pb[bi_s].ap(), op=OP.add),
                lambda e, hseg=hseg, seg=seg: e.tensor_copy(
                    out=Hbf[:, seg * 512:(seg + 1) * 512], in_=hseg),
            ]
            kk.chain("v", hfns, inc="sv")
            kk.guard(f"bank{bi_d}", "sv")
            kk.guard(f"bank{bi_s}", "sv")
        kk.guard("arowrep", "st")
        kk.guard("arow", "st")
        kk.guard("Xdt", "st")

    hdone = kk.cnt("sv")
    if p1max < 0:
        return
    kk.wait("y", "sv", hdone)
    kk.dma("y", io.cc_in[0:128, :], H.ap(), inc="sdy")
    kk.dma("y", io.cc_in[128:129, 0:64], b.SM4[0:64, 16:17], inc="sdy")
    agld = kk.cnt("sdy")
    kk.wait("g", "sdy", agld)
    kk.emit("g", lambda e: e.collective_compute(
        "AllGather", OP.bypass, replica_groups=[list(range(NCORES))],
        ins=[io.cc_in.opt()], outs=[io.ag_i.opt()]), inc="scc")
    kk.wait("g", "scc", kk.cnt("scc"))


def finish(kk):
    for sname in list(kk.sems.keys()):
        kk.wait("y", sname)


def build_L1(debug=(), upto=99, p1max=99):
    nc = bass.Bass("TRN2", target_bir_lowering=False, debug=False)
    io = declare_dram(nc, "L1", debug=debug)
    kk = K(nc)
    with kk.stack:
        b = alloc_bufs(nc, kk.stack)
        load_consts(kk, io, b, "L1")
        emit_ln(kk, io, b)
        if upto >= 2:
            emit_inproj(kk, io, b)
        if upto >= 3:
            dd = emit_dtpipe(kk, io, b)
        if upto >= 4:
            emit_pass1(kk, io, b, dd, p1max=p1max)
        if upto >= 3:
            tm1 = nc.dram_tensor("dbg_sm1", [128, 3072], F32,
                                 kind="ExternalOutput").ap()
            tm4 = nc.dram_tensor("dbg_sm4", [128, 96], F32,
                                 kind="ExternalOutput").ap()
            tm2 = nc.dram_tensor("dbg_xs", [128, 32768], BF16,
                                 kind="ExternalOutput").ap()
            tmb = nc.dram_tensor("dbg_bc", [128, 3072], BF16,
                                 kind="ExternalOutput").ap()
            for sname in ("ss", "sv", "st", "sg"):
                kk.wait("y", sname)
            kk.dma("y", tm1, b.SM1.ap(), inc="sdy")
            kk.dma("y", tm4, b.SM4.ap(), inc="sdy")
            kk.dma("y", tm2, b.AB2.ap()[:, 0:32768], inc="sdy")
            kk.dma("y", tmb, b.BC.ap(), inc="sdy")
        if upto == 2:
            t = nc.dram_tensor("dbg_ab1", [128, 33024], BF16,
                               kind="ExternalOutput").ap()
            t3 = nc.dram_tensor("dbg_ab3", [128, 8192], BF16,
                                kind="ExternalOutput").ap()
            t4 = nc.dram_tensor("dbg_mup", [128, NHC + 1], F32,
                                kind="ExternalOutput").ap()
            kk.dma("y", t4, b.MUP.ap(), inc="sdy")
            for sname in ("ss", "sv", "st"):
                kk.wait("y", sname)
            kk.dma("y", t, b.AB1.ap(), inc="sdy")
            kk.dma("y", t3, b.AB3.ap(), inc="sdy")
        if upto >= 4 and kk.cnt("scc") > 0:
            kk.wait("y", "scc", None)
            kk.dma("y", io.ag, io.ag_i, inc="sdy")
        finish(kk)
        kk.run()
    return nc


def kernel(**inputs):
    raise NotImplementedError



FQ = [22, 22, 21, 21]
FQ_OFF = [0, 22, 44, 65]


def emit_ffn(kk, io, b):
    AB1, AB2, AB3 = b.AB1, b.AB2, b.AB3
    xn = AB1
    kk.dma("g", AB1[:, 0:32768], io.xn_sp.transpose([1, 0, 2]))
    kk.dma("g", AB3[:, 0:4096], io.wga.transpose([1, 0, 2]))
    kk.dma("g", AB3[:, 4096:8192], io.wua.transpose([1, 0, 2]))
    lald = kk.cnt("sd")
    kk.chain("v", [lambda e: e.memset(b.BC.ap(), 0.0),
                   lambda e: e.memset(b.LT.ap(), 0.0),
                   lambda e: e.memset(b.CST.ap(), 0.0)], inc="sv")
    kk.wait("v", "sv", kk.cnt("sv"))
    kk.wait("t", "sd", lald)

    for li, (wof, bi) in enumerate(((0, 4096), (512, 5))):
        wof = 0 if li == 0 else 4096
        bi = 4 + li
        for th in range(2):
            kk.use("t", f"bank{bi}")
            for kc in range(KC):
                kk.emit("t", lambda e, wof=wof, bi=bi, th=th, kc=kc: e.matmul(
                    out=b.pb[bi].ap(),
                    lhsT=AB3[:, wof + kc * 128:wof + (kc + 1) * 128],
                    rhs=xn[:, kc * 1024 + th * 512:kc * 1024 + (th + 1) * 512],
                    start=(kc == 0), stop=(kc == KC - 1)),
                    inc=None if kc < KC - 1 else "st")
            kk.wait("v", "st", kk.cnt("st"))
            kk.emit("v", lambda e, li=li, bi=bi, th=th: e.tensor_copy(
                out=b.BC[0:16, li * 1024 + th * 512:li * 1024 + (th + 1) * 512],
                in_=b.pb[bi][0:16, 0:512]), inc="sv")
            kk.guard(f"bank{bi}", "sv")
    xa_done = kk.cnt("sv")
    kk.guard("gring0", "st")
    kk.guard("gring1", "st")
    xag = b.BC[:, 0:1024]
    xau = b.BC[:, 1024:2048]
    kk.wait("t", "sv", xa_done)

    gring = [AB3[:, 0:4096], AB3[:, 4096:8192]]
    uring = [AB2[:, 22528:26624], AB2[:, 26624:30720]]
    hid = AB2
    ltb = b.LT.ap().bitcast(BF16)
    gsb = b.F1[:, 0:512].bitcast(BF16)

    for q in range(4):
        nq = FQ[q]
        hid_last = 0
        for hqi in range(nq):
            hc = FQ_OFF[q] + hqi
            par = hc % 2
            gsl, usl = gring[par], uring[par]
            kk.use("g", f"gring{par}")
            kk.dma("g", gsl, io.wg[hc].transpose([1, 0, 2]))
            kk.use("g", f"uring{par}")
            kk.dma("g", usl, io.wu[hc].transpose([1, 0, 2]))
            kk.use("g", f"lorab{par}")
            kk.dma("g", ltb[:, par * 128:par * 128 + 128], io.wgb[hc])
            kk.dma("g", ltb[:, 256 + par * 128:256 + par * 128 + 128],
                   io.wub[hc])
            kk.use("g", f"wda{par}")
            kk.dma("g", b.CST[:, par * 128:(par + 1) * 128], io.wda[hc])
            wld = kk.cnt("sd")
            kk.wait("t", "sd", wld)
            for (sl, bb, xa, lbo) in ((gsl, 0, xag, par * 128),
                                      (usl, 2, xau, 256 + par * 128)):
                for th in range(2):
                    bi = bb + th
                    kk.use("t", f"bank{bi}")
                    for kc in range(KC):
                        kk.emit("t", lambda e, sl=sl, bi=bi, th=th, kc=kc:
                                e.matmul(
                                    out=b.pb[bi].ap(),
                                    lhsT=sl[:, kc * 128:(kc + 1) * 128],
                                    rhs=xn[:, kc * 1024 + th * 512:
                                           kc * 1024 + (th + 1) * 512],
                                    start=(kc == 0), stop=False), inc=None)
                    kk.emit("t", lambda e, bi=bi, th=th, xa=xa, lbo=lbo:
                            e.matmul(out=b.pb[bi].ap(),
                                     lhsT=ltb[:, lbo:lbo + 128],
                                     rhs=xa[:, th * 512:(th + 1) * 512],
                                     start=False, stop=True), inc="st")
            kk.guard(f"gring{par}", "st")
            kk.guard(f"uring{par}", "st")
            kk.guard(f"lorab{par}", "st")
            mmd = kk.cnt("st")
            kk.wait("s", "st", mmd)
            kk.use("s", "gsb")
            for th in range(2):
                kk.emit("s", lambda e, th=th: e.activation(
                    out=gsb[:, th * 512:(th + 1) * 512],
                    in_=b.pb[th].ap(), func=AF.Silu), inc="ss")
                kk.guard(f"bank{th}", "ss")
            sld = kk.cnt("ss")
            kk.wait("v", "ss", sld)
            for th in range(2):
                kk.emit("v", lambda e, th=th, hqi=hqi: e.tensor_tensor(
                    out=hid[:, hqi * 1024 + th * 512:hqi * 1024 + (th + 1) * 512],
                    in0=b.pb[2 + th].ap(),
                    in1=gsb[:, th * 512:(th + 1) * 512], op=OP.mult), inc="sv")
                kk.guard(f"bank{2 + th}", "sv")
            kk.guard("gsb", "sv")
            hid_last = kk.cnt("sv")
            kk.wait("t", "sv", hid_last)
            for th in range(2):
                kk.emit("t", lambda e, th=th, hqi=hqi, hc=hc, par=par: e.matmul(
                    out=b.pb[6 + th].ap(),
                    lhsT=b.CST[:, par * 128:(par + 1) * 128],
                    rhs=hid[:, hqi * 1024 + th * 512:hqi * 1024 + (th + 1) * 512],
                    start=(hc == 0), stop=(hc == NFH - 1), skip_group_check=True),
                    inc=None if hc < NFH - 1 else "st")
            kk.guard(f"wda{par}", "sv")

        if q == 3:
            kk.wait("v", "st", kk.cnt("st"))
            for th in range(2):
                kk.emit("v", lambda e, th=th: e.tensor_copy(
                    out=b.BC[0:16, 2048 + th * 512:2048 + (th + 1) * 512],
                    in_=b.pb[6 + th][0:16, 0:512]), inc="sv")
                kk.guard(f"bank{6 + th}", "sv")
            kk.wait("t", "sv", kk.cnt("sv"))
        xad = b.BC[:, 2048:3072]
        dring = [AB3[:, 0:2816], AB3[:, 2816:5632]]
        kk.wait("t", "sv", hid_last)
        for dc in range(32):
            par = dc % 2
            dsl = dring[par][:, 0:nq * 128]
            kk.use("g", f"gring0")
            kk.use("g", f"gring1")
            kk.use("g", f"dring{par}")
            kk.dma("g", dsl, io.wd[dc, FQ_OFF[q]:FQ_OFF[q] + nq]
                   .transpose([1, 0, 2]))
            if q == 3:
                kk.use("g", f"lorab{par}")
                kk.dma("g", ltb[:, par * 128:(par + 1) * 128], io.wdb[dc])
            dld = kk.cnt("sd")
            kk.wait("t", "sd", dld)
            for th in range(2):
                bi = 4 + th
                kk.use("t", f"bank{bi}")
                for hqi in range(nq):
                    last = (hqi == nq - 1) and q < 3
                    kk.emit("t", lambda e, dsl=dsl, bi=bi, th=th, hqi=hqi,
                            last=last: e.matmul(
                                out=b.pb[bi].ap(),
                                lhsT=dsl[:, hqi * 128:(hqi + 1) * 128],
                                rhs=hid[:, hqi * 1024 + th * 512:
                                        hqi * 1024 + (th + 1) * 512],
                                start=(hqi == 0), stop=last),
                            inc="st" if last else None)
                if q == 3:
                    kk.emit("t", lambda e, bi=bi, th=th, par=par: e.matmul(
                        out=b.pb[bi].ap(),
                        lhsT=ltb[:, par * 128:(par + 1) * 128],
                        rhs=xad[:, th * 512:(th + 1) * 512],
                        start=False, stop=True), inc="st")
            kk.guard(f"dring{par}", "st")
            if q == 3:
                kk.guard(f"lorab{par}", "st")
            dmm = kk.cnt("st")
            kk.wait("v", "st", dmm)
            kk.use("v", f"dstg{par}")
            for th in range(2):
                kk.emit("v", lambda e, th=th, par=par: e.tensor_copy(
                    out=b.H[:, par * 2048 + th * 512:par * 2048 + (th + 1) * 512],
                    in_=b.pb[4 + th].ap()), inc="sv")
                kk.guard(f"bank{4 + th}", "sv")
            kk.wait("g", "sv", kk.cnt("sv"))
            dst = io.acc[dc].transpose([1, 0, 2])
            stg = b.H.ap()[:, par * 2048:par * 2048 + 1024] \
                .rearrange("p (tc f) -> p tc f", f=128)
            if q == 0:
                kk.dma("g", dst, stg, inc="sda")
            else:
                kk.dma("g", dst, stg, accum_op=OP.add, inc="sda")
            kk.guard(f"dstg{par}", "sda")
    kk.wait("y", "sda")


def build_L2(debug=()):
    nc = bass.Bass("TRN2", target_bir_lowering=False, debug=False)
    io = declare_dram(nc, "L2", debug=debug)
    kk = K(nc)
    with kk.stack:
        b = alloc_bufs(nc, kk.stack)
        load_consts(kk, io, b, "L2")
        emit_ffn(kk, io, b)
        finish(kk)
        kk.run()
    return nc



def emit_pass2(kk, io, b):
    AB1, AB2, AB3 = b.AB1, b.AB2, b.AB3
    idn = b.CST[:, 0:128]
    idnf = b.CSTF[:, 0:128]
    norw = b.CSTF
    kk.dma("g", b.CSTF[:, 256:256 + KC], io.norw)
    kk.dma("g", b.CST[:, 0:128], io.idn)
    kk.dma("g", b.BC[:, 0:1024], io.cfm_sp)
    kk.dma("g", b.SM1[0:64, 0:1024], io.al_sp)
    kk.dma("g", b.SM3[0:128, 0:128], io.alpha)
    kk.dma("g", b.SM3[0:8, 128:129], io.pmask)
    kk.chain("v", [lambda e: e.memset(b.SM4.ap(), 0.0)], inc="sv")
    kk.wait("v", "sv", kk.cnt("sv"))
    kk.wait("y", "sv", kk.cnt("sv"))
    kk.dma("y", b.SM4[0:8, 0:64],
           io.ag.rearrange("(k r) d -> k r d", r=129)[:, 128, 0:64], inc="sdy")
    ld0 = kk.cnt("sd")
    ld0y = kk.cnt("sdy")
    kk.wait("t", "sd", ld0)
    kk.wait("t", "sdy", ld0y)
    kk.use("t", "bank6")
    kk.emit("t", lambda e: e.matmul(out=b.pb[6][:, 0:64],
                                    lhsT=b.SM3[0:128, 0:128],
                                    rhs=b.SM4[:, 0:64], start=True, stop=True),
            inc="st")
    lmd = kk.cnt("st")
    kk.wait("s", "st", lmd)
    kk.emit("s", lambda e: e.activation(out=b.SM3[0:8, 129:193],
                                        in_=b.pb[6][0:8, 0:64], func=AF.Exp),
            inc="ss")
    kk.guard("bank6", "ss")
    kk.wait("v", "ss", kk.cnt("ss"))
    kk.emit("v", lambda e: e.tensor_scalar(
        out=b.SM3[0:8, 129:129 + 64], in0=b.SM3[0:8, 129:129 + 64],
        scalar1=b.SM3[0:8, 128:129], scalar2=None, op0=OP.mult), inc="sv")
    w8done = kk.cnt("sv")

    kk.chain("v", [lambda e: e.memset(b.H.ap(), 0.0)], inc="sv")
    kk.wait("v", "sv", kk.cnt("sv"))
    wrep = AB1.ap()[0:1, 0:8192].bitcast(F32)
    brhs = AB1.ap()[:, 0:8192].bitcast(F32)
    sseg = [b.F1[:, 0:512], b.F1[:, 512:1024]]
    for j in range(NCORES):
        kk.wait("y", "sv", w8done)
        kk.use("y", "wrow")
        kk.dma("y", b.SM3[0:1, 193:257],
               b.SM3[0:8, 129:193][j:j + 1, :], inc="sdy")
        kk.wait("g", "sdy", kk.cnt("sdy"))
        kk.use("g", "wrep")
        kk.emit("g", lambda e: e.tensor_copy(
            out=wrep.rearrange("o (r p2) -> o r p2", p2=64),
            in_=b.SM3[0:1, 193:257].unsqueeze(2).broadcast_to([1, 64, 64])),
            inc="sg")
        wrd = kk.cnt("sg")
        for seg in range(8):
            ssl = sseg[seg % 2]
            kk.use("y", f"sseg{seg % 2}")
            kk.dma("y", ssl, io.ag[129 * j:129 * j + 128,
                                   seg * 512:(seg + 1) * 512], inc="sdy")
            sld = kk.cnt("sdy")
            bi = seg % 2
            kk.use("t", f"bank{bi}")
            kk.wait("t", "sg", wrd)
            kk.emit("t", lambda e, bi=bi, seg=seg: e.matmul(
                out=b.pb[bi].ap(), lhsT=b.CSTF[:, 128:256],
                rhs=brhs[:, seg * 512:(seg + 1) * 512], start=True, stop=True),
                inc="st")
            wbd = kk.cnt("st")
            kk.wait("v", "st", wbd)
            kk.wait("v", "sdy", sld)
            kk.chain("v", [
                lambda e, ssl=ssl, bi=bi: e.tensor_tensor(
                    out=ssl, in0=ssl, in1=b.pb[bi].ap(), op=OP.mult),
                lambda e, ssl=ssl, seg=seg: e.tensor_tensor(
                    out=b.H[:, seg * 512:(seg + 1) * 512],
                    in0=b.H[:, seg * 512:(seg + 1) * 512], in1=ssl, op=OP.add),
            ], inc="sv")
            kk.guard(f"bank{bi}", "sv")
            kk.guard(f"sseg{seg % 2}", "sv")
        kk.guard("wrep", "st")
        kk.guard("wrow", "sg")
    hind = kk.cnt("sv")
    kk.wait("v", "sv", hind)
    Hbf = AB1[:, 8192:12288]
    kk.chain("v", [lambda e: e.tensor_copy(out=Hbf, in_=b.H.ap())], inc="sv")
    hbfd = kk.cnt("sv")

    ybuf = AB1[:, 16384:20480]
    zbuf = AB1[:, 20480:24576]
    scrib = AB1[:, 24576:28672]
    yn_tm = AB1[:, 28672:32768]
    wtm = b.LT[:, 0:64]
    yt = b.H
    u = b.F1[:, 0:512]
    SM = b.SM2

    for c in range(TC):
        kk.use("y", "ybuf")
        kk.dma("y", ybuf, io.y_sp[c], inc="sdy")
        kk.use("y", "zbuf")
        kk.dma("y", zbuf, io.z_sp[c], inc="sdy")
        yld = kk.cnt("sdy")
        kk.wait("s", "sv", None)
        kk.use("s", "wfm")
        kk.emit("s", lambda e, c=c: e.activation(
            out=b.F1[0:64, 1024:1152], in_=b.SM1[0:64, c * 128:(c + 1) * 128],
            func=AF.Exp), inc="ss")
        kk.wait("t", "ss", kk.cnt("ss"))
        kk.use("t", "bank7")
        kk.emit("t", lambda e: e.transpose(out=b.pb[7][:, 0:64],
                                           in_=b.F1[0:64, 1024:1152],
                                           identity=idnf[0:64, 0:64]), inc="st")
        kk.wait("v", "st", kk.cnt("st"))
        kk.use("v", "wtm")
        kk.emit("v", lambda e: e.tensor_copy(out=wtm, in_=b.pb[7][:, 0:64]),
                inc="sv")
        kk.guard("bank7", "sv")
        kk.guard("wfm", "st")
        wtmd = kk.cnt("sv")

        kk.wait("t", "sv", hbfd)
        for g8 in range(8):
            bi = 2 + g8 % 2
            kk.use("t", f"bank{bi}")
            kk.emit("t", lambda e, bi=bi, c=c, g8=g8: e.matmul(
                out=b.pb[bi].ap(), lhsT=b.BC[:, c * 128:(c + 1) * 128],
                rhs=Hbf[:, g8 * 512:(g8 + 1) * 512], start=True, stop=True),
                inc="st")
            yid = kk.cnt("st")
            kk.wait("v", "st", yid)
            kk.wait("v", "sv", wtmd)
            kk.wait("v", "sdy", yld)
            kk.chain("v", [
                lambda e, bi=bi, g8=g8: e.tensor_tensor(
                    out=u.rearrange("p (r hd) -> p r hd", hd=64),
                    in0=b.pb[bi].ap().rearrange("p (r hd) -> p r hd", hd=64),
                    in1=wtm[:, g8 * 8:(g8 + 1) * 8].unsqueeze(2)
                    .broadcast_to([128, 8, 64]), op=OP.mult),
                lambda e, g8=g8: e.tensor_tensor(
                    out=yt.ap()[:, g8 * 512:(g8 + 1) * 512],
                    in0=ybuf[:, g8 * 512:(g8 + 1) * 512], in1=u, op=OP.add),
                lambda e, g8=g8: e.tensor_tensor(
                    out=yt.ap()[:, g8 * 512:(g8 + 1) * 512],
                    in0=yt.ap()[:, g8 * 512:(g8 + 1) * 512],
                    in1=zbuf[:, g8 * 512:(g8 + 1) * 512], op=OP.mult),
            ], inc="sv")
            kk.guard(f"bank{bi}", "sv")
        gd = kk.cnt("sv")
        kk.guard("ybuf", "sv")
        kk.guard("zbuf", "sv")
        kk.guard("wtm", "sv")

        kk.wait("s", "sv", gd)
        kk.use("s", "scrib")
        kk.emit("s", lambda e: e.activation(out=scrib, in_=yt.ap(),
                                            func=AF.Square,
                                            accum_out=SM[:, 0:1]), inc="ss")
        kk.wait("v", "ss", kk.cnt("ss"))
        kk.chain("v", [
            lambda e: e.tensor_scalar(out=SM[:, 1:2], in0=SM[:, 0:1],
                                      scalar1=1.0 / D, scalar2=EPS,
                                      op0=OP.mult, op1=OP.add),
            lambda e: e.reciprocal(out=SM[:, 2:3], in_=SM[:, 1:2]),
        ], inc="sv")
        kk.wait("s", "sv", kk.cnt("sv"))
        kk.emit("s", lambda e: e.activation(out=SM[:, 3:4], in_=SM[:, 2:3],
                                            func=AF.Sqrt), inc="ss")
        kk.wait("v", "ss", kk.cnt("ss"))
        kk.use("v", "yn_tm")
        kk.emit("v", lambda e: e.tensor_scalar(out=yn_tm, in0=yt.ap(),
                                               scalar1=SM[:, 3:4], scalar2=None,
                                               op0=OP.mult), inc="sv")
        ynd = kk.cnt("sv")
        kk.guard("scrib", "sv")

        kk.wait("t", "sv", ynd)
        for dq in range(8):
            bi = 4 + dq % 2
            kk.use("t", f"bank{bi}")
            for j in range(4):
                dc = dq * 4 + j
                kk.emit("t", lambda e, dc=dc, j=j, bi=bi: e.transpose(
                    out=pbf(b.pb[bi])[:, j * 128:(j + 1) * 128],
                    in_=yn_tm[:, dc * 128:(dc + 1) * 128], identity=idn),
                    inc=None if j < 3 else "st")
            kk.wait("v", "st", kk.cnt("st"))
            kk.emit("v", lambda e, dq=dq, c=c, bi=bi: e.tensor_copy(
                out=AB2.ap()[:, 0:32768]
                .rearrange("p (dc t) -> p dc t", t=1024)
                [:, dq * 4:(dq + 1) * 4, c * 128:(c + 1) * 128],
                in_=pbf(b.pb[bi])[:, 0:512].rearrange("p (j t) -> p j t", t=128)),
                inc="sv")
            kk.guard(f"bank{bi}", "sv")
        kk.guard("yn_tm", "st")

    ynfd = kk.cnt("sv")
    kk.wait("v", "sv", ynfd)
    for dc in range(KC):
        kk.emit("v", lambda e, dc=dc: e.tensor_scalar(
            out=AB2.ap()[:, dc * 1024:(dc + 1) * 1024],
            in0=AB2.ap()[:, dc * 1024:(dc + 1) * 1024],
            scalar1=norw[:, 256 + dc:257 + dc], scalar2=None, op0=OP.mult),
            inc="sv")
    ynsc = kk.cnt("sv")
    kk.wait("t", "sv", ynsc)

    oring = [AB3[:, 0:4096], AB3[:, 4096:8192]]
    for oc in range(32):
        par = oc % 2
        osl = oring[par]
        kk.use("g", f"oring{par}")
        kk.dma("g", osl, io.wop[oc].transpose([1, 0, 2]))
        old = kk.cnt("sd")
        kk.wait("t", "sd", old)
        for th in range(2):
            bi = par * 2 + th
            kk.use("t", f"bank{bi}")
            for kc in range(KC):
                kk.emit("t", lambda e, osl=osl, bi=bi, th=th, kc=kc: e.matmul(
                    out=b.pb[bi].ap(), lhsT=osl[:, kc * 128:(kc + 1) * 128],
                    rhs=AB2.ap()[:, kc * 1024 + th * 512:
                                 kc * 1024 + (th + 1) * 512],
                    start=(kc == 0), stop=(kc == KC - 1)),
                    inc=None if kc < KC - 1 else "st")
        kk.guard(f"oring{par}", "st")
        omm = kk.cnt("st")
        kk.wait("v", "st", omm)
        kk.use("v", f"ostg{par}")
        for th in range(2):
            kk.emit("v", lambda e, th=th, par=par: e.tensor_copy(
                out=b.F1[:, 1024 + par * 1024 + th * 512:
                         1024 + par * 1024 + (th + 1) * 512]
                if False else
                b.F1[:, 1088 + par * 512:1088 + par * 512 + 512]
                if False else b.F1[:, 1536 + th * 512:1536 + (th + 1) * 512],
                in_=b.pb[par * 2 + th].ap()), inc="sv") if False else None
        for th in range(2):
            kk.emit("v", lambda e, th=th, par=par: e.tensor_copy(
                out=scrib.bitcast(F32)[:, par * 1024 + th * 512:
                                       par * 1024 + (th + 1) * 512]
                if False else
                AB1.ap()[:, 24576 + par * 4096:24576 + par * 4096 + 4096]
                .bitcast(F32)[:, th * 512:(th + 1) * 512],
                in_=b.pb[par * 2 + th].ap()), inc="sv")
            kk.guard(f"bank{par * 2 + th}", "sv")
        kk.wait("g", "sv", kk.cnt("sv"))
        stg = AB1.ap()[:, 24576 + par * 4096:24576 + par * 4096 + 4096] \
            .bitcast(F32)[:, 0:1024].rearrange("p (tc f) -> p tc f", f=128)
        kk.dma("g", io.acc[oc].transpose([1, 0, 2]), stg, accum_op=OP.add,
               inc="sda")
        kk.guard(f"ostg{par}", "sda")
    kk.wait("y", "sda")
    if io.acc2 is not io.acc:
        kk.wait("y", "sda")
        kk.dma("y", io.acc2, io.acc, inc="sdy")


def build_L3(debug=()):
    nc = bass.Bass("TRN2", target_bir_lowering=False, debug=False)
    io = declare_dram(nc, "L3", debug=debug)
    kk = K(nc)
    with kk.stack:
        b = alloc_bufs(nc, kk.stack)
        load_consts(kk, io, b, "L3")
        emit_pass2(kk, io, b)
        finish(kk)
        kk.run()
    return nc



def emit_final(kk, io, b):
    AB1, AB2 = b.AB1, b.AB2
    idnf = b.CSTF[:, 0:128]
    acc_src = io.acc2
    xr = [AB1.ap()[:, 0:8192].bitcast(F32), AB1.ap()[:, 8192:16384].bitcast(F32)]
    ar = [AB2.ap()[:, 0:8192].bitcast(F32), AB2.ap()[:, 8192:16384].bitcast(F32)]
    ost = b.H
    for tc in range(TC):
        par = tc % 2
        kk.use("y", f"xr{par}")
        kk.dma("y", xr[par],
               io.x.rearrange("(tc p) d -> tc p d", p=128)[tc], inc="sdy")
        kk.use("y", f"ar{par}")
        kk.dma("y", ar[par], acc_src[:, tc].transpose([1, 0, 2]), inc="sdy")
        ld = kk.cnt("sdy")
        kk.wait("t", "sdy", ld)
        kk.use("v", "ost")
        for dq in range(8):
            bi = 4 + dq % 2
            kk.use("t", f"bank{bi}")
            for j in range(4):
                dc = dq * 4 + j
                kk.emit("t", lambda e, dc=dc, j=j, bi=bi, par=par: e.transpose(
                    out=b.pb[bi][:, j * 128:(j + 1) * 128],
                    in_=ar[par].rearrange("p (dc f) -> p dc f", f=128)[:, dc, :]
                    if False else ar[par][:, dc * 128:(dc + 1) * 128],
                    identity=idnf), inc=None if j < 3 else "st")
            tdn = kk.cnt("st")
            kk.wait("v", "st", tdn)
            kk.emit("v", lambda e, dq=dq, bi=bi, par=par: e.scalar_tensor_tensor(
                out=ost.ap()[:, dq * 512:(dq + 1) * 512],
                in0=b.pb[bi].ap(), scalar=0.0,
                in1=xr[par][:, dq * 512:(dq + 1) * 512],
                op0=OP.add, op1=OP.add), inc="sv")
            kk.guard(f"bank{bi}", "sv")
        odone = kk.cnt("sv")
        kk.guard(f"xr{par}", "sv")
        kk.guard(f"ar{par}", "st")
        kk.wait("y", "sv", odone)
        kk.dma("y", io.out.rearrange("(tc p) d -> tc p d", p=128)[tc],
               ost.ap(), inc="sdo")
        kk.guard("ost", "sdo")
    kk.wait("y", "sdo")


def build_L4(debug=()):
    nc = bass.Bass("TRN2", target_bir_lowering=False, debug=False)
    io = declare_dram(nc, "L4", debug=debug)
    kk = K(nc)
    with kk.stack:
        b = alloc_bufs(nc, kk.stack)
        load_consts(kk, io, b, "L4")
        emit_final(kk, io, b)
        finish(kk)
        kk.run()
    return nc



K1KEYS = ("bc128", "x", "xh", "wip", "convw", "mup", "lnw", "alog", "dtb", "d_rep",
          "tri", "idn", "idnf", "ones1f", "epsc")
K2KEYS = ("bc128", "wg", "wu", "wd", "wga", "wua", "wgb", "wub", "wda", "wdb",
          "tri", "idn", "idnf", "ones1f", "epsc")
K3KEYS = ("bc128", "wop", "norw", "alpha", "pmask", "tri", "idn", "idnf", "ones1f",
          "epsc")
K4KEYS = ("bc128", "x", "tri", "idn", "idnf", "ones1f", "epsc")


def kernel(**inputs):
    per_core = relayout(inputs)
    nc1 = build_L1()
    r1 = run_bass_kernel_spmd(
        nc1, [{k: m[k] for k in K1KEYS} for m in per_core],
        core_ids=list(range(NCORES)))
    nc2 = build_L2()
    maps2 = []
    for c in range(NCORES):
        m = {k: per_core[c][k] for k in K2KEYS}
        m["xn_sp"] = np.asarray(r1.results[c]["xn_sp"])
        maps2.append(m)
    r2 = run_bass_kernel_spmd(nc2, maps2, core_ids=list(range(NCORES)))
    nc3 = build_L3()
    maps3 = []
    for c in range(NCORES):
        m = {k: per_core[c][k] for k in K3KEYS}
        for k in ("z_sp", "y_sp", "al_sp", "cfm_sp", "ag"):
            m[k] = np.asarray(r1.results[c][k])
        m["acc"] = np.asarray(r2.results[c]["acc"])
        maps3.append(m)
    r3 = run_bass_kernel_spmd(nc3, maps3, core_ids=list(range(NCORES)))
    nc4 = build_L4()
    maps4 = []
    for c in range(NCORES):
        m = {k: per_core[c][k] for k in K4KEYS}
        m["acc2"] = np.asarray(r3.results[c]["acc2"])
        maps4.append(m)
    r4 = run_bass_kernel_spmd(nc4, maps4, core_ids=list(range(NCORES)))
    out = np.concatenate([np.asarray(r4.results[c]["out"])[None]
                          for c in range(NCORES)], 0)
    return np.ascontiguousarray(out.reshape(2, 4096, D)).astype(np.float32)


def barrier(kk):
    for e in ENGINES:
        for sname in list(kk.sems.keys()):
            kk.wait(e, sname)


def build_fused():
    nc = bass.Bass("TRN2", target_bir_lowering=False, debug=False)
    io = declare_dram(nc, None)
    kk = K(nc)
    with kk.stack:
        b = alloc_bufs(nc, kk.stack)
        load_consts(kk, io, b, None)
        emit_ln(kk, io, b)
        emit_inproj(kk, io, b)
        dd = emit_dtpipe(kk, io, b)
        emit_pass1(kk, io, b, dd)
        barrier(kk)
        emit_ffn(kk, io, b)
        barrier(kk)
        emit_pass2(kk, io, b)
        barrier(kk)
        emit_final(kk, io, b)
        finish(kk)
        kk.run()
    return nc


KALL = tuple(sorted(set(K1KEYS + K2KEYS + K3KEYS + K4KEYS + ("norw",))))


def kernel(**inputs):
    per_core = relayout(inputs)
    nc = build_fused()
    res = run_bass_kernel_spmd(
        nc, [{k: m[k] for k in KALL} for m in per_core],
        core_ids=list(range(NCORES)))
    out = np.concatenate([np.asarray(res.results[c]["out"])[None]
                          for c in range(NCORES)], 0)
    return np.ascontiguousarray(out.reshape(2, 4096, D)).astype(np.float32)


# version 03 (final)
# speedup vs baseline: 1.1104x; 1.1104x over previous
import contextlib

import numpy as np
import ml_dtypes

import concourse.bass as bass
import concourse.mybir as mybir
from concourse.bass_utils import run_bass_kernel_spmd
from concourse.alu_op_type import AluOpType as OP

F32 = mybir.dt.float32
BF16 = mybir.dt.bfloat16
AF = mybir.ActivationFunctionType
BF = ml_dtypes.bfloat16

T = 1024
TC = 8
D = 4096
KC = 32
NHC = 67
NXBC = 34
NH = 64
HD = 64
DH = 11008
NFH = 86
EPS = 1e-5
NCORES = 8

ENGINES = ("t", "v", "s", "g", "y")


class K:

    def __init__(self, nc):
        self.nc = nc
        self.prog = {e: [] for e in ENGINES}
        self.sems = {}
        self.stack = contextlib.ExitStack()
        self.guards = {}

    def sem(self, name):
        if name not in self.sems:
            h = self.stack.enter_context(self.nc.semaphore(name))
            self.sems[name] = [h, 0]
        return self.sems[name]

    def wait(self, eng, name, val=None):
        s = self.sem(name)
        v = s[1] if val is None else val
        if v <= 0:
            return
        h = s[0]
        self.prog[eng].append(lambda e, h=h, v=v: e.wait_ge(h, v))

    def emit(self, eng, fn, inc=None, n=1):
        if inc is None:
            self.prog[eng].append(fn)
            return None
        s = self.sem(inc)
        s[1] += n
        h = s[0]
        if eng == "t":
            self.prog[eng].append(fn)

            def thunk(e, h=h, n=n):
                e.drain().then_inc(h, n)
        else:
            def thunk(e, fn=fn, h=h, n=n):
                fn(e).then_inc(h, n)
        self.prog[eng].append(thunk)
        return s[1]

    def dma(self, eng, out, in_, inc="sd", **kw):
        return self.emit(
            eng,
            lambda e, out=out, in_=in_, kw=kw: e.dma_start(out=out, in_=in_, **kw),
            inc=inc, n=16)

    def cnt(self, name):
        return self.sem(name)[1]

    def use(self, eng, res):
        g = self.guards.get(res)
        if g:
            self.wait(eng, g[0], g[1])

    def guard(self, res, sem_name):
        self.guards[res] = (sem_name, self.cnt(sem_name))

    def chain(self, eng, fns, inc):
        for i, fn in enumerate(fns):
            if i > 0:
                self.wait(eng, inc)
            self.emit(eng, fn, inc=inc)

    def run(self):
        with self.nc.Block() as block:
            @block.tensor
            def _(e):
                for th in self.prog["t"]:
                    th(e)

            @block.vector
            def _(e):
                for th in self.prog["v"]:
                    th(e)

            @block.scalar
            def _(e):
                for th in self.prog["s"]:
                    th(e)

            @block.gpsimd
            def _(e):
                for th in self.prog["g"]:
                    th(e)

            @block.sync
            def _(e):
                for th in self.prog["y"]:
                    th(e)



def relayout(inputs):
    C = np.ascontiguousarray
    _ = inputs.get("L0_attn_residual")
    x = np.asarray(inputs["L0_attn_hidden_states"], np.float32).reshape(8192, D)
    wip = np.asarray(inputs["L0_attn_in_proj_w"], np.float32)
    wop = np.asarray(inputs["L0_attn_out_proj_w"], np.float32)
    convw = np.asarray(inputs["L0_attn_conv_w"], np.float32)
    alog = np.asarray(inputs["L0_attn_A_log"], np.float32)
    dtb = np.asarray(inputs["L0_attn_dt_bias"], np.float32)
    Dv = np.asarray(inputs["L0_attn_D"], np.float32)
    lnw = np.asarray(inputs["L0_attn_ln_w"], np.float32)
    norw = np.asarray(inputs["L0_attn_norm_w"], np.float32)
    mup = np.asarray(inputs["L0_attn_mup_vector"], np.float32)
    wg = np.asarray(inputs["L0_ffn_gate_w"], np.float32)
    wu = np.asarray(inputs["L0_ffn_up_w"], np.float32)
    wd = np.asarray(inputs["L0_ffn_down_w"], np.float32)
    ga = np.asarray(inputs["L0_ffn_gate_a"], np.float32)
    gb = np.asarray(inputs["L0_ffn_gate_b"], np.float32)
    ua = np.asarray(inputs["L0_ffn_up_a"], np.float32)
    ub = np.asarray(inputs["L0_ffn_up_b"], np.float32)
    da = np.asarray(inputs["L0_ffn_down_a"], np.float32)
    db = np.asarray(inputs["L0_ffn_down_b"], np.float32)

    def pad_cols(a, n):
        return np.concatenate([a, np.zeros((a.shape[0], n - a.shape[1]), a.dtype)], 1)

    shared = dict(
        wip=C(pad_cols(wip, NHC * 128).reshape(KC, 128, NHC, 128).transpose(2, 0, 1, 3)),
        wop=C(wop.reshape(KC, 128, 32, 128).transpose(2, 0, 1, 3)),
        wg=C(wg.T.reshape(KC, 128, NFH, 128).transpose(2, 0, 1, 3)),
        wu=C(wu.T.reshape(KC, 128, NFH, 128).transpose(2, 0, 1, 3)),
        wd=C(wd.T.reshape(NFH, 128, 32, 128).transpose(2, 0, 1, 3)),
        wga=C(np.concatenate([ga.T.reshape(KC, 128, 16),
                              np.zeros((KC, 128, 112), np.float32)], 2)),
        wua=C(np.concatenate([ua.T.reshape(KC, 128, 16),
                              np.zeros((KC, 128, 112), np.float32)], 2)),
        wgb=C(np.concatenate([gb.T.reshape(16, NFH, 128).transpose(1, 0, 2),
                              np.zeros((NFH, 112, 128), np.float32)], 1)),
        wub=C(np.concatenate([ub.T.reshape(16, NFH, 128).transpose(1, 0, 2),
                              np.zeros((NFH, 112, 128), np.float32)], 1)),
        wda=C(np.concatenate([da.T.reshape(NFH, 128, 16),
                              np.zeros((NFH, 128, 112), np.float32)], 2)),
        wdb=C(np.concatenate([db.T.reshape(16, 32, 128).transpose(1, 0, 2),
                              np.zeros((32, 112, 128), np.float32)], 1)),
        convw=C(convw.reshape(NXBC, 128, 4)),
        mup=C(pad_cols(mup[None, :], NHC * 128).reshape(NHC, 128).T),
        lnw=C(lnw.reshape(KC, 128).T),
        norw=C(norw.reshape(KC, 128).T),
        d_rep=C(np.broadcast_to(np.repeat(Dv, HD)[None, :], (128, 4096)).copy()),
        alog=C(alog[:, None]),
        dtb=C(dtb[:, None]),
        tri=np.triu(np.ones((128, 128), np.float32)).astype(BF),
        idn=np.eye(128, dtype=np.float32).astype(BF),
        idnf=np.eye(128, dtype=np.float32),
        ones1f=np.ones((1, 128), np.float32),
        bc128=np.concatenate([np.ones((1, 128), np.float32),
                              np.zeros((127, 128), np.float32)], 0),
        epsc=np.full((128, 1), EPS, np.float32),
    )

    per_core = []
    for c in range(NCORES):
        t0 = c * T
        halo = np.zeros((3, D), np.float32)
        if c % 4 != 0:
            halo = C(x[t0 - 3:t0])
        alpha = np.zeros((128, 128), np.float32)
        pmask = np.zeros((8, 1), np.float32)
        bbase, q = (c // 4) * 4, c % 4
        for j in range(q):
            pmask[bbase + j, 0] = 1.0
            for i in range(j + 1, q):
                alpha[bbase + i, bbase + j] = 1.0
        m = dict(shared)
        m.update(x=C(x[t0:t0 + T]), xh=halo, alpha=alpha, pmask=pmask)
        per_core.append(m)
    return per_core



class NS:
    def __init__(self, **kw):
        self.__dict__.update(kw)


def declare_dram(nc, launch, debug=()):
    io = NS()
    L = launch
    io._dbg = debug

    def inp(name, shape, ty=F32):
        setattr(io, name, nc.dram_tensor(name, shape, ty, kind="ExternalInput").ap())

    def scr(name, shape, ty, prod, cons):
        if L is None:
            kind = "ExternalOutput" if name in debug else "Internal"
            ap = nc.dram_tensor(name, shape, ty, kind=kind).ap()
        elif L == prod:
            ap = nc.dram_tensor(name, shape, ty, kind="ExternalOutput").ap()
        elif L in cons:
            ap = nc.dram_tensor(name, shape, ty, kind="ExternalInput").ap()
        else:
            ap = None
        setattr(io, name, ap)

    if L in (None, "L1", "L4"):
        inp("x", [T, D])
    if L in (None, "L1"):
        inp("xh", [3, D])
        inp("wip", [NHC, KC, 128, 128])
        inp("convw", [NXBC, 128, 4])
        inp("mup", [128, NHC])
        inp("lnw", [128, KC])
        inp("alog", [NH, 1])
        inp("dtb", [NH, 1])
        inp("d_rep", [128, D])
    if L in (None, "L2"):
        inp("wg", [NFH, KC, 128, 128])
        inp("wu", [NFH, KC, 128, 128])
        inp("wd", [32, NFH, 128, 128])
        inp("wga", [KC, 128, 128])
        inp("wua", [KC, 128, 128])
        inp("wgb", [NFH, 128, 128])
        inp("wub", [NFH, 128, 128])
        inp("wda", [NFH, 128, 128])
        inp("wdb", [32, 128, 128])
    if L in (None, "L3"):
        inp("wop", [32, KC, 128, 128])
        inp("norw", [128, KC])
        inp("alpha", [128, 128])
        inp("pmask", [8, 1])
    inp("tri", [128, 128], BF16)
    inp("idn", [128, 128], BF16)
    inp("idnf", [128, 128])
    inp("ones1f", [1, 128])
    inp("bc128", [128, 128])
    inp("epsc", [128, 1])

    scr("xn_sp", [KC, 128, T], BF16, "L1", ["L2"])
    scr("z_sp", [TC, 128, D], BF16, "L1", ["L3"])
    scr("y_sp", [TC, 128, D], BF16, "L1", ["L3"])
    scr("al_sp", [NH, T], F32, "L1", ["L3"])
    scr("cfm_sp", [128, T], BF16, "L1", ["L3"])
    if L is None:
        io.acc = nc.dram_tensor(
            "acc", [32, TC, 128, 128], F32,
            kind="ExternalOutput" if "acc" in debug else "Internal").ap()
        io.acc2 = io.acc
    else:
        scr("acc", [32, TC, 128, 128], F32, "L2", ["L3"])
        if L == "L3":
            io.acc2 = nc.dram_tensor("acc2", [32, TC, 128, 128], F32,
                                     kind="ExternalOutput").ap()
        elif L == "L4":
            io.acc2 = nc.dram_tensor("acc2", [32, TC, 128, 128], F32,
                                     kind="ExternalInput").ap()
    if L in (None, "L1"):
        io.cc_in = nc.dram_tensor("cc_in", [129, D], F32).ap()
    if L in (None, "L1"):
        io.ag_i = nc.dram_tensor("ag_i", [129 * NCORES, D], F32).ap()
    scr("ag", [129 * NCORES, D], F32, "L1", ["L3"])
    if L is None:
        io.ag = io.ag_i
    if L in (None, "L4"):
        io.out = nc.dram_tensor("out", [T, D], F32, kind="ExternalOutput").ap()
    io.zfm_dbg = None
    if "zfm" in debug:
        io.zfm_dbg = nc.dram_tensor("zfm_dbg", [32, 128, T], BF16,
                                    kind="ExternalOutput").ap()
    io.ps_dbg = None
    if "ps" in debug:
        io.ps_dbg = nc.dram_tensor("ps_dbg", [128, 1024], F32,
                                   kind="ExternalOutput").ap()
    return io


def alloc_bufs(nc, st):
    b = NS()
    E = st.enter_context
    b.AB1 = E(nc.sbuf_tensor("AB1", [128, 33024], BF16))
    b.AB2 = E(nc.sbuf_tensor("AB2", [128, 33024], BF16))
    b.AB3 = E(nc.sbuf_tensor("AB3", [128, 8192], BF16))
    b.H = E(nc.sbuf_tensor("Hbuf", [128, 4096], F32))
    b.F1 = E(nc.sbuf_tensor("F1", [128, 2624], F32))
    b.SM1 = E(nc.sbuf_tensor("SM1", [128, 3072], F32))
    b.SM2 = E(nc.sbuf_tensor("SM2", [128, 16], F32))
    b.SM3 = E(nc.sbuf_tensor("SM3", [128, 288], F32))
    b.SM4 = E(nc.sbuf_tensor("SM4", [128, 96], F32))
    b.BC = E(nc.sbuf_tensor("BCbuf", [128, 3072], BF16))
    b.LT = E(nc.sbuf_tensor("LTbuf", [128, 256], F32))
    b.TRI = E(nc.sbuf_tensor("TRIbuf", [128, 128], BF16))
    b.CST = E(nc.sbuf_tensor("CSTbuf", [128, 256], BF16))
    b.CSTF = E(nc.sbuf_tensor("CSTFbuf", [128, 288], F32))
    b.MUP = E(nc.sbuf_tensor("MUPbuf", [128, NHC + 1], F32))
    b.pb = [E(nc.psum_tensor(f"pb{i}", [128, 512], F32)) for i in range(8)]
    return b


def pbf(bank):
    return bank.ap().bitcast(BF16)



def load_consts(kk, io, b, launch):
    kk.dma("g", b.CST[:, 0:128], io.idn)
    kk.dma("g", b.CST[:, 128:256], io.tri)
    kk.dma("g", b.CSTF[:, 0:128], io.idnf)
    kk.dma("g", b.CSTF[:, 128:256], io.bc128)
    kk.dma("g", b.SM2[:, 8:9], io.epsc)
    if launch in (None, "L1"):
        kk.dma("g", b.CSTF[:, 256:256 + KC], io.lnw)
        kk.dma("g", b.MUP[:, 0:NHC], io.mup)
        kk.dma("g", b.SM3[:, 0:NXBC * 4],
               io.convw.transpose([1, 0, 2]))
        kk.dma("g", b.SM3[0:NH, 140:141], io.alog)
        kk.dma("g", b.SM3[0:NH, 142:143], io.dtb)
    return kk.cnt("sd")



def emit_ln(kk, io, b):
    AB1, AB2, SM = b.AB1, b.AB2, b.SM2
    ab2f = AB2.ap().bitcast(F32)
    xring = [ab2f[:, 0:4096], ab2f[:, 4096:8192]]
    xn_tm = AB2[:, 16384:20480]
    xnr_tm = AB2[:, 20480:24576]
    scr1 = AB2[:, 24576:28672]
    scr2 = AB2[:, 28672:32768]
    stg = [b.F1[:, 0:256].bitcast(BF16), b.F1[:, 256:512].bitcast(BF16)]
    idn = b.CST[:, 0:128]
    lnw = b.CSTF

    cons = kk.cnt("sd")
    kk.wait("s", "sd", cons)
    kk.wait("v", "sd", cons)
    for tc in range(TC + 1):
        haloT = tc == TC
        rows = 3 if haloT else 128
        r_ = slice(0, rows)
        src = io.xh if haloT else io.x.rearrange("(tc p) d -> tc p d", p=128)[tc]
        xt = xring[tc % 2][r_, :]
        kk.use("y", f"xring{tc % 2}")
        kk.dma("y", xt, src, inc="sdy")
        ld = kk.cnt("sdy")

        kk.wait("s", "sdy", ld)
        kk.use("s", "smstat")
        kk.chain("s", [
            lambda e, xt=xt, o=scr1[r_, :], a=SM[r_, 0:1]:
                e.activation(out=o, in_=xt, func=AF.Copy, accum_out=a),
            lambda e, xt=xt, o=scr2[r_, :], a=SM[r_, 1:2]:
                e.activation(out=o, in_=xt, func=AF.Square, accum_out=a),
        ], inc="ss")
        kk.wait("v", "ss", kk.cnt("ss"))
        kk.chain("v", [
            lambda e, r_=r_: e.tensor_scalar(
                out=SM[r_, 2:3], in0=SM[r_, 0:1], scalar1=1.0 / D,
                scalar2=None, op0=OP.mult),
            lambda e, r_=r_: e.tensor_tensor(
                out=SM[r_, 6:7], in0=SM[r_, 2:3], in1=SM[r_, 2:3], op=OP.mult),
            lambda e, r_=r_: e.scalar_tensor_tensor(
                out=SM[r_, 3:4], in0=SM[r_, 1:2], scalar=1.0 / D,
                in1=SM[r_, 6:7], op0=OP.mult, op1=OP.subtract),
        ], inc="sv")
        kk.guard("smstat", "sv")
        kk.wait("s", "sv", kk.cnt("sv"))
        kk.chain("s", [
            lambda e, r_=r_: e.activation(out=SM[r_, 6:7], in_=SM[r_, 3:4],
                                          func=AF.Ln, bias=SM[r_, 8:9]),
            lambda e, r_=r_: e.activation(out=SM[r_, 4:5], in_=SM[r_, 6:7],
                                          func=AF.Exp, scale=-0.5),
            lambda e, r_=r_: e.activation(out=SM[r_, 7:8], in_=SM[r_, 6:7],
                                          func=AF.Exp, scale=-1.0),
        ], inc="ss")
        kk.wait("v", "ss", kk.cnt("ss"))
        kk.chain("v", [
            lambda e, r_=r_: e.scalar_tensor_tensor(
                out=SM[r_, 6:7], in0=SM[r_, 3:4], scalar=0.0,
                in1=SM[r_, 7:8], op0=OP.add, op1=OP.mult),
            lambda e, r_=r_: e.tensor_scalar(
                out=SM[r_, 6:7], in0=SM[r_, 6:7], scalar1=EPS,
                scalar2=None, op0=OP.add),
        ], inc="sv")
        kk.wait("s", "sv", kk.cnt("sv"))
        kk.chain("s", [
            lambda e, r_=r_: e.activation(out=SM[r_, 5:6], in_=SM[r_, 6:7],
                                          func=AF.Ln),
            lambda e, r_=r_: e.activation(out=SM[r_, 5:6], in_=SM[r_, 5:6],
                                          func=AF.Exp, scale=-0.5),
        ], inc="ss")
        kk.wait("v", "ss", kk.cnt("ss"))
        kk.use("v", "xtmbufs")
        kk.chain("v", [
            lambda e, r_=r_: e.tensor_tensor(out=SM[r_, 5:6], in0=SM[r_, 4:5],
                                             in1=SM[r_, 5:6], op=OP.mult),
            lambda e, xt=xt, r_=r_: e.tensor_scalar(
                out=xn_tm[r_, :], in0=xt, scalar1=SM[r_, 2:3],
                scalar2=SM[r_, 4:5], op0=OP.subtract, op1=OP.mult),
            lambda e, xt=xt, r_=r_: e.tensor_scalar(
                out=xnr_tm[r_, :], in0=xt, scalar1=SM[r_, 2:3],
                scalar2=SM[r_, 5:6], op0=OP.subtract, op1=OP.mult),
        ], inc="sv")
        tsd = kk.cnt("sv")
        kk.guard(f"xring{tc % 2}", "sv")

        kk.wait("t", "sv", tsd)
        kk.wait("t", "sd", cons)
        if haloT:
            for dc in range(KC):
                bi = 4 + (dc % 2)
                kk.use("t", f"bank{bi}")
                kk.emit("t", lambda e, i=xnr_tm[0:3, dc * 128:(dc + 1) * 128],
                        pv=pbf(b.pb[bi]), idn=idn:
                        e.transpose(out=pv[:, 0:3], in_=i, identity=idn[0:3, 0:3]),
                        inc="st")
                kk.wait("s", "st", kk.cnt("st"))
                kk.emit("s", lambda e, pv=pbf(b.pb[bi]), dc=dc:
                        e.activation(out=AB1[:, 32768 + dc * 8:32768 + dc * 8 + 3],
                                     in_=pv[:, 0:3], func=AF.Copy,
                                     scale=lnw[:, 256 + dc:257 + dc]), inc="ss")
                kk.guard(f"bank{bi}", "ss")
        else:
            for half in range(2):
                tm = xn_tm if half == 0 else xnr_tm
                for dq in range(8):
                    bi = (6 if half == 0 else 4) + (dq % 2)
                    kk.use("t", f"bank{bi}")
                    for j in range(4):
                        dc = dq * 4 + j
                        kk.emit("t", lambda e, i=tm[:, dc * 128:(dc + 1) * 128],
                                pv=pbf(b.pb[bi]), j=j, idn=idn:
                                e.transpose(out=pv[:, j * 128:(j + 1) * 128], in_=i,
                                            identity=idn),
                                inc=None if j < 3 else "st")
                    tdn = kk.cnt("st")
                    if half == 0:
                        sg = stg[dq % 2]
                        kk.wait("v", "st", tdn)
                        kk.use("v", f"stg{dq % 2}")
                        kk.emit("v", lambda e, pv=pbf(b.pb[bi]), sg=sg:
                                e.tensor_copy(out=sg, in_=pv[:, 0:512]), inc="sv")
                        kk.guard(f"bank{bi}", "sv")
                        kk.wait("y", "sv", kk.cnt("sv"))
                        dst = io.xn_sp[dq * 4:dq * 4 + 4, :,
                                       tc * 128:(tc + 1) * 128] \
                            .transpose([1, 0, 2])
                        kk.dma("y", dst, sg, inc="sdy")
                        kk.guard(f"stg{dq % 2}", "sdy")
                    else:
                        kk.wait("s", "st", tdn)
                        for j in range(4):
                            dc = dq * 4 + j
                            kk.emit("s", lambda e, pv=pbf(b.pb[bi]), j=j, dc=dc,
                                    tc=tc:
                                    e.activation(
                                        out=AB1[:, dc * 1024 + tc * 128:
                                                dc * 1024 + (tc + 1) * 128],
                                        in_=pv[:, j * 128:(j + 1) * 128],
                                        func=AF.Copy,
                                        scale=lnw[:, 256 + dc:257 + dc]),
                                    inc="ss")
                        kk.guard(f"bank{bi}", "ss")
            kk.guard("xtmbufs", "st")



def emit_inproj(kk, io, b):
    AB1, AB2, AB3 = b.AB1, b.AB2, b.AB3
    idn = b.CST[:, 0:128]
    wring = [AB3[:, 0:4096], AB3[:, 4096:8192]]
    xs_tm = AB2.ap()[:, 0:32768].rearrange("p (tc c) -> p tc c", c=4096)

    kk.wait("t", "ss", None)
    for hc in range(NHC):
        wid = 64 if hc == NHC - 1 else 128
        slot = wring[hc % 2]
        kk.use("g", f"wring{hc % 2}")
        kk.dma("g", slot, io.wip[hc].transpose([1, 0, 2]))
        wld = kk.cnt("sd")
        kk.wait("t", "sd", wld)

        is_xbc = 32 <= hc < 66
        segs = [(0, 512, (hc % 2) * 2 + 0), (512, 1024, (hc % 2) * 2 + 1)]
        if is_xbc:
            segs.append((-3, 0, 4 + hc % 2))
        for (c0, c1, bi) in segs:
            bank = b.pb[bi]
            kk.use("t", f"bank{bi}")
            for kc in range(KC):
                if c0 < 0:
                    rhs = AB1[:, 32768 + kc * 8:32768 + kc * 8 + 3]
                    out = bank[0:wid, 0:3]
                else:
                    rhs = AB1[:, kc * 1024 + c0:kc * 1024 + c1]
                    out = bank[0:wid, 0:(c1 - c0)]
                kk.emit("t", lambda e, out=out, w=slot[:, kc * 128:kc * 128 + wid],
                        rhs=rhs, st=(kc == 0), sp=(kc == KC - 1):
                        e.matmul(out=out, lhsT=w, rhs=rhs, start=st, stop=sp),
                        inc=None if kc < KC - 1 else "st")
        kk.guard(f"wring{hc % 2}", "st")
        mmdone = kk.cnt("st")
        mupc = b.MUP[:, hc:hc + 1]

        if hc < 32:
            zfm = b.BC[:, 2048:3072]
            kk.wait("v", "st", mmdone)
            if io.ps_dbg is not None and hc == 0:
                for (c0, c1, bi) in segs:
                    kk.emit("v", lambda e, bi=bi, c0=c0, c1=c1: e.tensor_copy(
                        out=b.H[:, c0:c1], in_=b.pb[bi][:, 0:512]), inc="sv")
                kk.wait("y", "sv", kk.cnt("sv"))
                kk.dma("y", io.ps_dbg, b.H.ap()[:, 0:1024], inc="sdy")
            kk.use("v", "zpre")
            for (c0, c1, bi) in segs:
                kk.emit("v", lambda e, bi=bi, c0=c0, c1=c1, mupc=mupc:
                        e.tensor_scalar(
                            out=b.F1[:, 1056 + c0:1056 + c1],
                            in0=b.pb[bi][:, 0:512],
                            scalar1=mupc, scalar2=None, op0=OP.mult), inc="sv")
                kk.guard(f"bank{bi}", "sv")
            kk.wait("s", "sv", kk.cnt("sv"))
            kk.use("s", "zfm")
            kk.emit("s", lambda e: e.activation(
                out=zfm, in_=b.F1[:, 1056:2080], func=AF.Silu), inc="ss")
            kk.guard("zpre", "ss")
            zdone = kk.cnt("ss")
            if io.zfm_dbg is not None:
                kk.wait("y", "ss", zdone)
                kk.dma("y", io.zfm_dbg[hc], zfm, inc="sdy")
                kk.guard("zfm", "sdy")
            kk.wait("t", "ss", zdone)
            for g in range(2):
                bi = 6 + g
                kk.use("t", f"bank{bi}")
                for j in range(4):
                    tcb = g * 4 + j
                    kk.emit("t", lambda e, i=zfm[:, tcb * 128:(tcb + 1) * 128],
                            pv=pbf(b.pb[bi]), j=j, idn=idn:
                            e.transpose(out=pv[:, j * 128:(j + 1) * 128], in_=i,
                                        identity=idn), inc=None if j < 3 else "st")
                kk.wait("v", "st", kk.cnt("st"))
                kk.use("v", f"zstg{g}")
                sg = b.F1[:, 512 + g * 256:768 + g * 256].bitcast(BF16)
                kk.emit("v", lambda e, pv=pbf(b.pb[bi]), sg=sg:
                        e.tensor_copy(out=sg, in_=pv[:, 0:512]), inc="sv")
                kk.guard(f"bank{bi}", "sv")
                kk.wait("y", "sv", kk.cnt("sv"))
                dst = io.z_sp[g * 4:(g + 1) * 4, :, hc * 128:(hc + 1) * 128] \
                    .transpose([1, 0, 2])
                kk.dma("y", dst, sg, inc="sdy")
                kk.guard(f"zstg{g}", "sdy")
            kk.guard("zfm", "st")
        elif hc < 66:
            xc = hc - 32
            ping = b.F1[:, 1056 * (hc % 2):1056 * (hc % 2) + 1056]
            kk.wait("s", "st", mmdone)
            kk.use("s", f"convp{hc % 2}")
            for (c0, c1, bi) in segs:
                if c0 < 0:
                    o, i = ping[:, 0:3], b.pb[bi][:, 0:3]
                else:
                    o, i = ping[:, 3 + c0:3 + c1], b.pb[bi][:, 0:512]
                kk.emit("s", lambda e, o=o, i=i, mupc=mupc:
                        e.activation(out=o, in_=i, func=AF.Copy, scale=mupc),
                        inc="ss")
                kk.guard(f"bank{bi}", "ss")
            evd = kk.cnt("ss")
            cacc = b.SM1[:, 2048:3072]
            kk.wait("v", "ss", evd)
            kk.use("v", "cacc")

            cfns = [lambda e, ping=ping, xc=xc, cacc=cacc: e.tensor_scalar(
                out=cacc, in0=ping[:, 0:1024],
                scalar1=b.SM3[:, xc * 4:xc * 4 + 1], scalar2=None, op0=OP.mult)]
            for k2 in range(1, 4):
                cfns.append(lambda e, ping=ping, xc=xc, cacc=cacc, k2=k2:
                            e.scalar_tensor_tensor(
                                out=cacc, in0=ping[:, k2:k2 + 1024],
                                scalar=b.SM3[:, xc * 4 + k2:xc * 4 + k2 + 1],
                                in1=cacc, op0=OP.mult, op1=OP.add))
            kk.chain("v", cfns, inc="sv")
            cvd = kk.cnt("sv")
            kk.guard(f"convp{hc % 2}", "sv")
            kk.wait("s", "sv", cvd)
            if xc < 32:
                xbfm = b.F1[:, 2112:2624].bitcast(BF16)
                kk.use("s", "xbfm")
                kk.emit("s", lambda e, xbfm=xbfm, cacc=cacc:
                        e.activation(out=xbfm, in_=cacc, func=AF.Silu), inc="ss")
                kk.guard("cacc", "ss")
                kk.wait("t", "ss", kk.cnt("ss"))
                for g in range(2):
                    bi = 6 + g
                    kk.use("t", f"bank{bi}")
                    for j in range(4):
                        tcb = g * 4 + j
                        kk.emit("t", lambda e, i=xbfm[:, tcb * 128:(tcb + 1) * 128],
                                pv=pbf(b.pb[bi]), j=j, idn=idn:
                                e.transpose(out=pv[:, j * 128:(j + 1) * 128], in_=i,
                                            identity=idn),
                                inc=None if j < 3 else "st")
                    kk.wait("v", "st", kk.cnt("st"))
                    kk.emit("v", lambda e, pv=pbf(b.pb[bi]), g=g, xc=xc:
                            e.tensor_copy(
                                out=xs_tm[:, g * 4:(g + 1) * 4,
                                          xc * 128:(xc + 1) * 128],
                                in_=pv[:, 0:512].rearrange("p (j c) -> p j c",
                                                           c=128)),
                            inc="sv")
                    kk.guard(f"bank{bi}", "sv")
                kk.guard("xbfm", "st")
            else:
                dst = b.BC[:, 0:1024] if xc == 32 else b.BC[:, 1024:2048]
                kk.emit("s", lambda e, dst=dst, cacc=cacc:
                        e.activation(out=dst, in_=cacc, func=AF.Silu), inc="ss")
                kk.guard("cacc", "ss")
                sdn = kk.cnt("ss")
                if xc == 32:
                    kk.wait("t", "ss", sdn)
                    for g in range(2):
                        bi = 6 + g
                        kk.use("t", f"bank{bi}")
                        for j in range(4):
                            tcb = g * 4 + j
                            kk.emit("t", lambda e,
                                    i=dst[:, tcb * 128:(tcb + 1) * 128],
                                    pv=pbf(b.pb[bi]), j=j, idn=idn:
                                    e.transpose(out=pv[:, j * 128:(j + 1) * 128],
                                                in_=i, identity=idn),
                                    inc=None if j < 3 else "st")
                        kk.wait("v", "st", kk.cnt("st"))
                        kk.emit("v", lambda e, pv=pbf(b.pb[bi]), g=g:
                                e.tensor_copy(out=b.BC[:, 2048 + g * 512:
                                                       2048 + (g + 1) * 512],
                                              in_=pv[:, 0:512]), inc="sv")
                        kk.guard(f"bank{bi}", "sv")
                else:
                    kk.wait("y", "ss", sdn)
                    kk.dma("y", io.cfm_sp, dst, inc="sdy")
        else:
            kk.wait("s", "st", mmdone)
            for (c0, c1, bi) in segs:
                kk.emit("s", lambda e, bi=bi, c0=c0, c1=c1, mupc=mupc:
                        e.activation(out=b.SM1[0:64, c0:c1],
                                     in_=b.pb[bi][0:64, 0:512],
                                     func=AF.Copy, scale=mupc[0:64, :]), inc="ss")
                kk.guard(f"bank{bi}", "ss")



def emit_dtpipe(kk, io, b):
    SM1, SM4, SM3 = b.SM1, b.SM4, b.SM3
    dtr = SM1[0:64, 0:1024]
    tmp = SM1[0:64, 1024:2048]
    dtsp = SM1[0:64, 2048:3072]
    lp = [SM1[0:64, 0:1024], SM1[0:64, 1024:2048]]
    dtbc = SM3[0:64, 142:143]

    kk.wait("s", "ss", None)
    kk.chain("s", [
        lambda e: e.activation(out=SM3[0:64, 141:142], in_=SM3[0:64, 140:141],
                               func=AF.Exp),
        lambda e: e.activation(out=tmp, in_=dtr, func=AF.Abs, bias=dtbc),
        lambda e: e.activation(out=tmp, in_=tmp, func=AF.Exp, scale=-1.0),
        lambda e: e.activation(out=tmp, in_=tmp, func=AF.Ln, bias=1.0),
    ], inc="ss")
    sdn = kk.cnt("ss")
    kk.wait("v", "ss", sdn)
    kk.chain("v", [
        lambda e: e.tensor_scalar(out=SM3[0:64, 141:142], in0=SM3[0:64, 141:142],
                                  scalar1=-1.0, scalar2=None, op0=OP.mult),
        lambda e: e.tensor_scalar(out=dtsp, in0=dtr, scalar1=dtbc, scalar2=0.0,
                                  op0=OP.add, op1=OP.max),
        lambda e: e.tensor_tensor(out=dtsp, in0=dtsp, in1=tmp, op=OP.add),
        lambda e: e.tensor_scalar(out=lp[1], in0=dtsp,
                                  scalar1=SM3[0:64, 141:142], scalar2=None,
                                  op0=OP.mult),
    ], inc="sv")

    cur = 1
    for rnd in range(7):
        k = 1 << rnd
        src_, dst = lp[cur], lp[1 - cur]

        def round_(e, src_=src_, dst=dst, k=k):
            s3 = src_.rearrange("p (c t) -> p c t", t=128)
            d3 = dst.rearrange("p (c t) -> p c t", t=128)
            e.tensor_copy(out=d3[:, :, 0:k], in_=s3[:, :, 0:k])
            return e.tensor_tensor(out=d3[:, :, k:128], in0=s3[:, :, k:128],
                                   in1=s3[:, :, 0:128 - k], op=OP.add)
        kk.wait("v", "sv")
        kk.emit("v", round_, inc="sv")
        cur = 1 - cur
    lfin = lp[cur]

    kk.wait("v", "sv")
    kk.emit("v", lambda e: e.tensor_copy(
        out=SM4[0:64, 0:8],
        in_=lfin.rearrange("p (c t) -> p c t", t=128)[:, :, 127:128]
        .rearrange("p c t -> p (c t)")), inc="sv")
    kk.emit("v", lambda e: e.memset(SM4[0:64, 8:16], 0.0), inc="sv")
    ofns = []
    for c in range(1, 8):
        ofns.append(lambda e, c=c: e.tensor_tensor(
            out=SM4[0:64, 8 + c:9 + c], in0=SM4[0:64, 7 + c:8 + c],
            in1=SM4[0:64, c - 1:c], op=OP.add))
    ofns.append(lambda e: e.tensor_tensor(
        out=SM4[0:64, 16:17], in0=SM4[0:64, 15:16], in1=SM4[0:64, 7:8],
        op=OP.add))
    aloc = SM1[0:64, 1024:2048]

    def aloc_all(e):
        last = None
        for c in range(8):
            last = e.tensor_scalar(out=aloc[:, c * 128:(c + 1) * 128],
                                   in0=lfin[:, c * 128:(c + 1) * 128],
                                   scalar1=SM4[0:64, 8 + c:9 + c], scalar2=None,
                                   op0=OP.add)
        return last
    ofns.append(aloc_all)
    kk.wait("v", "sv")
    kk.chain("v", ofns, inc="sv")
    vdn = kk.cnt("sv")
    kk.wait("s", "sv", vdn)
    kk.emit("s", lambda e: e.activation(out=SM4[0:64, 24:32], in_=SM4[0:64, 0:8],
                                        func=AF.Exp), inc="ss")
    kk.wait("y", "sv", vdn)
    kk.dma("y", io.al_sp, SM1[0:64, 1024:2048], inc="sdy")
    return NS(aloc=SM1[0:64, 1024:2048], dtsp=dtsp, lfin=lfin)



def emit_pass1(kk, io, b, dd, p1max=99):
    AB1, AB2 = b.AB1, b.AB2
    idnf = b.CSTF[:, 0:128]
    ones1f = b.CSTF[0:1, 128:256]
    tri = b.CST[:, 128:256]
    H = b.H
    xs_tm = AB2.ap()[:, 0:32768].rearrange("p (tc c) -> p tc c", c=4096)
    Xdt = AB1[:, 0:4096]
    Db = AB1[:, 4096:8192]
    Hbf = AB1[:, 8192:12288]
    Mg = [AB1[:, 12288:13312], AB1[:, 13312:14336]]
    Eg = AB1[:, 14336:14848]
    ych = AB1[:, 14848:18944]
    arow = AB1.ap()[0:1, 18944:27136].bitcast(F32)
    xsD = AB1[:, 27136:31232]
    dt_tm = b.LT[:, 0:64]
    dece_tm = b.LT[:, 64:128]
    decin_tm = b.LT[:, 128:192]
    na_tm = b.LT[:, 192:256]
    dece_f = b.F1[0:64, 0:128]
    decin_f = b.F1[0:64, 128:256]
    na_f = b.F1[0:64, 256:384]
    Dm = b.F1[:, 512:1024]
    u = b.F1[:, 1024:1536]

    ipdone = kk.cnt("st")
    for eng in ("v", "s", "g", "y"):
        kk.wait(eng, "st", ipdone)
    if p1max >= -1:
        kk.dma("g", Db, io.d_rep)
    dbld = kk.cnt("sd")
    kk.wait("v", "sv", None)

    kk.chain("v", [lambda e: e.memset(H.ap(), 0.0),
                   lambda e: e.memset(Hbf, 0.0)], inc="sv")
    kk.wait("v", "sv", kk.cnt("sv"))

    for c in range(TC):
        if p1max < 1:
            continue
        ccol = slice(c * 128, (c + 1) * 128)
        kk.wait("s", "sv", None)
        kk.wait("s", "ss", None)
        kk.use("s", "decef")

        kk.emit("s", lambda e, c=c: e.activation(
            out=dece_f, in_=dd.lfin[:, c * 128:(c + 1) * 128],
            func=AF.Exp, scale=-1.0, bias=b.SM4[0:64, c:c + 1]), inc="ss")
        kk.emit("s", lambda e, c=c: e.activation(
            out=decin_f, in_=dd.lfin[:, c * 128:(c + 1) * 128], func=AF.Exp),
            inc="ss")
        sfmd = kk.cnt("ss")
        kk.wait("v", "ss", sfmd)
        kk.use("v", "naf")
        kk.emit("v", lambda e, c=c: e.tensor_scalar(
            out=na_f, in0=dd.aloc[:, c * 128:(c + 1) * 128], scalar1=-1.0,
            scalar2=None, op0=OP.mult), inc="sv")
        fmdone = kk.cnt("sv")

        kk.wait("t", "sv", fmdone)
        kk.wait("t", "ss", sfmd)
        for (src, dst, tag) in ((dd.dtsp[:, ccol], dt_tm, "dt"),
                                (dece_f, dece_tm, "de"),
                                (decin_f, decin_tm, "di"), (na_f, na_tm, "na")):
            kk.use("t", "bank7")
            kk.emit("t", lambda e, src=src: e.transpose(
                out=b.pb[7][:, 0:64], in_=src, identity=idnf[0:64, 0:64]),
                inc="st")
            kk.wait("v", "st", kk.cnt("st"))
            kk.use("v", "lt" + tag)
            kk.emit("v", lambda e, dst=dst: e.tensor_copy(
                out=dst, in_=b.pb[7][:, 0:64]), inc="sv")
            kk.guard("bank7", "sv")
            kk.guard("lt_src_" + tag, "st")
        kk.guard("decef", "st")
        kk.guard("naf", "st")
        tmv = kk.cnt("sv")

        if p1max < 2:
            continue
        kk.use("y", "arow")
        kk.dma("y", arow[0:1, 0:4096].rearrange("o (r t) -> o r t", t=128),
               dd.aloc[0:32, ccol], inc="sdy")
        arow_h = [kk.cnt("sdy"), None]

        kk.wait("g", "sv", tmv)
        kk.wait("g", "sd", dbld)
        kk.use("g", "Xdt")
        kk.emit("g", lambda e, c=c: e.tensor_tensor(
            out=Xdt.rearrange("p (r hd) -> p r hd", hd=64),
            in0=xs_tm[:, c, :].rearrange("p (r hd) -> p r hd", hd=64),
            in1=dt_tm.unsqueeze(2).broadcast_to([128, 64, 64]), op=OP.mult),
            inc="sg")
        xdt_done = kk.cnt("sg")
        kk.guard("ltdt", "sg")
        kk.use("g", "xsD")
        kk.emit("g", lambda e, c=c: e.tensor_tensor(
            out=xsD, in0=xs_tm[:, c, :], in1=Db, op=OP.mult), inc="sg")
        xsd_done = kk.cnt("sg")

        if p1max < 3:
            continue
        kk.use("t", "bank6")
        kk.emit("t", lambda e, c=c: e.matmul(
            out=b.pb[6][:, 0:128], lhsT=b.BC[:, c * 128:(c + 1) * 128],
            rhs=b.BC[:, 1024 + c * 128:1024 + (c + 1) * 128], start=True,
            stop=True), inc="st")
        kk.wait("v", "st", kk.cnt("st"))
        kk.use("v", "cbt")
        kk.emit("v", lambda e: e.tensor_tensor(out=b.TRI[:, 0:128],
                                               in0=b.pb[6][:, 0:128], in1=tri,
                                               op=OP.mult), inc="sv")
        kk.guard("bank6", "sv")
        cbt_done = kk.cnt("sv")

        if p1max < 4:
            continue
        for g8 in range(8):
            mslot = Mg[g8 % 2]
            for g4i in range(2):
                g4 = g8 * 2 + g4i
                if g4 == 8:
                    kk.wait("y", "st", kk.cnt("st"))
                    kk.dma("y", arow[0:1, 0:4096]
                           .rearrange("o (r t) -> o r t", t=128),
                           dd.aloc[32:64, ccol], inc="sdy")
                    arow_h[1] = kk.cnt("sdy")
                bi = g4 % 2
                kk.use("t", f"bank{bi}")
                kk.wait("t", "sdy", arow_h[0] if g4 < 8 else arow_h[1])
                kk.emit("t", lambda e, g4=g4, bi=bi: e.matmul(
                    out=b.pb[bi].ap(), lhsT=ones1f,
                    rhs=arow[0:1, (g4 % 8) * 512:(g4 % 8 + 1) * 512],
                    start=True, stop=True), inc="st")
                abd = kk.cnt("st")
                kk.wait("v", "st", abd)
                kk.wait("v", "sv", tmv)
                kk.use("v", "Dm")
                for hh in range(4):
                    r = g4 * 4 + hh
                    kk.emit("v", lambda e, bi=bi, hh=hh, r=r: e.tensor_scalar(
                        out=Dm[:, hh * 128:(hh + 1) * 128],
                        in0=b.pb[bi][:, hh * 128:(hh + 1) * 128],
                        scalar1=na_tm[:, r:r + 1], scalar2=87.0,
                        op0=OP.add, op1=OP.min), inc="sv")
                kk.guard(f"bank{bi}", "sv")
                dmd = kk.cnt("sv")
                kk.wait("s", "sv", dmd)
                kk.use("s", "Eg")
                kk.emit("s", lambda e: e.activation(out=Eg, in_=Dm, func=AF.Exp),
                        inc="ss")
                kk.guard("Dm", "ss")
                ed = kk.cnt("ss")
                kk.wait("v", "ss", ed)
                kk.wait("v", "sv", cbt_done)
                kk.use("v", f"mslot{g8 % 2}")
                kk.emit("v", lambda e, mslot=mslot, g4i=g4i: e.tensor_tensor(
                    out=mslot[:, g4i * 512:(g4i + 1) * 512]
                    .rearrange("p (h c) -> p h c", c=128),
                    in0=Eg.rearrange("p (h c) -> p h c", c=128),
                    in1=b.TRI[:, 0:128].unsqueeze(1).broadcast_to([128, 4, 128]),
                    op=OP.mult), inc="sv")
                kk.guard("Eg", "sv")
            mdone = kk.cnt("sv")

            if p1max < 5:
                continue
            bi_a = 2 + g8 % 2
            bi_i = 4 + g8 % 2
            kk.wait("t", "sv", mdone)
            kk.wait("t", "sg", xdt_done)
            kk.use("t", f"bank{bi_a}")
            kk.use("t", f"bank{bi_i}")
            kk.emit("t", lambda e, bi_i=bi_i, c=c, g8=g8: e.matmul(
                out=b.pb[bi_i].ap(),
                lhsT=b.BC[:, 1024 + c * 128:1024 + (c + 1) * 128],
                rhs=Hbf[:, g8 * 512:(g8 + 1) * 512], start=True, stop=True),
                inc="st")
            for hh in range(8):
                r = g8 * 8 + hh
                kk.emit("t", lambda e, hh=hh, r=r, mslot=mslot, bi_a=bi_a:
                        e.matmul(out=b.pb[bi_a][:, hh * 64:(hh + 1) * 64],
                                 lhsT=mslot[:, hh * 128:(hh + 1) * 128],
                                 rhs=Xdt[:, r * 64:(r + 1) * 64],
                                 start=(hh == 0), stop=(hh == 7)),
                        inc=None if hh < 7 else "st")
            kk.guard(f"mslot{g8 % 2}", "st")
            ydone = kk.cnt("st")

            kk.wait("v", "st", ydone)
            kk.wait("v", "sg", xsd_done)
            kk.use("v", "u")
            kk.use("v", "ych")

            yfns = [
                lambda e, bi_i=bi_i, g8=g8: e.tensor_tensor(
                    out=u.rearrange("p (r hd) -> p r hd", hd=64),
                    in0=b.pb[bi_i].ap().rearrange("p (r hd) -> p r hd", hd=64),
                    in1=decin_tm[:, g8 * 8:(g8 + 1) * 8].unsqueeze(2)
                    .broadcast_to([128, 8, 64]), op=OP.mult),
                lambda e, bi_a=bi_a: e.scalar_tensor_tensor(
                    out=u, in0=b.pb[bi_a].ap(), scalar=0.0, in1=u, op0=OP.add,
                    op1=OP.add),
                lambda e, g8=g8: e.tensor_tensor(
                    out=ych[:, g8 * 512:(g8 + 1) * 512], in0=u,
                    in1=xsD[:, g8 * 512:(g8 + 1) * 512], op=OP.add),
            ]
            kk.chain("v", yfns, inc="sv")
            kk.guard(f"bank{bi_a}", "sv")
            kk.guard(f"bank{bi_i}", "sv")
            kk.guard("lt" + "di", "sv")
        yevd = kk.cnt("sv")
        kk.wait("y", "sv", yevd)
        kk.dma("y", io.y_sp[c], ych, inc="sdy")
        kk.guard("ych", "sdy")
        kk.guard("xsD", "sdy")

        if p1max < 6:
            continue
        kk.wait("g", "st", kk.cnt("st"))
        kk.emit("g", lambda e: e.tensor_tensor(
            out=Xdt.rearrange("p (r hd) -> p r hd", hd=64),
            in0=Xdt.rearrange("p (r hd) -> p r hd", hd=64),
            in1=dece_tm.unsqueeze(2).broadcast_to([128, 64, 64]), op=OP.mult),
            inc="sg")
        kk.guard("ltde", "sg")
        xdtd = kk.cnt("sg")

        kk.use("y", "sm4row")
        if c == 0:
            kk.wait("y", "ss", None)
        kk.dma("y", b.SM4[0:1, 32:96], b.SM4[0:64, 24 + c:25 + c], inc="sdy")
        kk.wait("g", "sdy", kk.cnt("sdy"))
        kk.wait("g", "st", kk.cnt("st"))
        kk.emit("g", lambda e: e.tensor_copy(
            out=arow[0:1, 0:4096].rearrange("o (r j) -> o r j", j=64),
            in_=b.SM4[0:1, 32:96].unsqueeze(2).broadcast_to([1, 64, 64])),
            inc="sg")
        kk.guard("sm4row", "sg")
        dtrd = kk.cnt("sg")

        if p1max < 7:
            continue
        for seg in range(8):
            bi_s = 4 + seg % 2
            bi_d = seg % 2
            kk.wait("t", "sg", xdtd)
            kk.use("t", f"bank{bi_s}")
            kk.emit("t", lambda e, seg=seg, c=c: e.matmul(
                out=b.pb[4 + seg % 2].ap(),
                lhsT=b.BC[:, 2048 + c * 128:2048 + (c + 1) * 128],
                rhs=Xdt[:, seg * 512:(seg + 1) * 512], start=True, stop=True),
                inc="st")
            kk.wait("t", "sg", dtrd)
            kk.use("t", f"bank{bi_d}")
            kk.emit("t", lambda e, seg=seg: e.matmul(
                out=b.pb[seg % 2].ap(), lhsT=ones1f,
                rhs=arow[0:1, seg * 512:(seg + 1) * 512], start=True, stop=True),
                inc="st")
            smm = kk.cnt("st")
            kk.wait("v", "st", smm)
            hseg = H[:, seg * 512:(seg + 1) * 512]

            hfns = [
                lambda e, hseg=hseg, bi_d=bi_d: e.tensor_tensor(
                    out=hseg, in0=hseg, in1=b.pb[bi_d].ap(), op=OP.mult),
                lambda e, hseg=hseg, bi_s=bi_s: e.tensor_tensor(
                    out=hseg, in0=hseg, in1=b.pb[bi_s].ap(), op=OP.add),
                lambda e, hseg=hseg, seg=seg: e.tensor_copy(
                    out=Hbf[:, seg * 512:(seg + 1) * 512], in_=hseg),
            ]
            kk.chain("v", hfns, inc="sv")
            kk.guard(f"bank{bi_d}", "sv")
            kk.guard(f"bank{bi_s}", "sv")
        kk.guard("arowrep", "st")
        kk.guard("arow", "st")
        kk.guard("Xdt", "st")

    hdone = kk.cnt("sv")
    if p1max < 0:
        return
    kk.wait("y", "sv", hdone)
    kk.dma("y", io.cc_in[0:128, :], H.ap(), inc="sdy")
    kk.dma("y", io.cc_in[128:129, 0:64], b.SM4[0:64, 16:17], inc="sdy")
    agld = kk.cnt("sdy")
    kk.wait("g", "sdy", agld)
    kk.emit("g", lambda e: e.collective_compute(
        "AllGather", OP.bypass, replica_groups=[list(range(NCORES))],
        ins=[io.cc_in.opt()], outs=[io.ag_i.opt()]), inc="scc")
    kk.wait("g", "scc", kk.cnt("scc"))


def finish(kk):
    for sname in list(kk.sems.keys()):
        kk.wait("y", sname)


def build_L1(debug=(), upto=99, p1max=99):
    nc = bass.Bass("TRN2", target_bir_lowering=False, debug=False)
    io = declare_dram(nc, "L1", debug=debug)
    kk = K(nc)
    with kk.stack:
        b = alloc_bufs(nc, kk.stack)
        load_consts(kk, io, b, "L1")
        emit_ln(kk, io, b)
        if upto >= 2:
            emit_inproj(kk, io, b)
        if upto >= 3:
            dd = emit_dtpipe(kk, io, b)
        if upto >= 4:
            emit_pass1(kk, io, b, dd, p1max=p1max)
        if upto >= 3:
            tm1 = nc.dram_tensor("dbg_sm1", [128, 3072], F32,
                                 kind="ExternalOutput").ap()
            tm4 = nc.dram_tensor("dbg_sm4", [128, 96], F32,
                                 kind="ExternalOutput").ap()
            tm2 = nc.dram_tensor("dbg_xs", [128, 32768], BF16,
                                 kind="ExternalOutput").ap()
            tmb = nc.dram_tensor("dbg_bc", [128, 3072], BF16,
                                 kind="ExternalOutput").ap()
            for sname in ("ss", "sv", "st", "sg"):
                kk.wait("y", sname)
            kk.dma("y", tm1, b.SM1.ap(), inc="sdy")
            kk.dma("y", tm4, b.SM4.ap(), inc="sdy")
            kk.dma("y", tm2, b.AB2.ap()[:, 0:32768], inc="sdy")
            kk.dma("y", tmb, b.BC.ap(), inc="sdy")
        if upto == 2:
            t = nc.dram_tensor("dbg_ab1", [128, 33024], BF16,
                               kind="ExternalOutput").ap()
            t3 = nc.dram_tensor("dbg_ab3", [128, 8192], BF16,
                                kind="ExternalOutput").ap()
            t4 = nc.dram_tensor("dbg_mup", [128, NHC + 1], F32,
                                kind="ExternalOutput").ap()
            kk.dma("y", t4, b.MUP.ap(), inc="sdy")
            for sname in ("ss", "sv", "st"):
                kk.wait("y", sname)
            kk.dma("y", t, b.AB1.ap(), inc="sdy")
            kk.dma("y", t3, b.AB3.ap(), inc="sdy")
        if upto >= 4 and kk.cnt("scc") > 0:
            kk.wait("y", "scc", None)
            kk.dma("y", io.ag, io.ag_i, inc="sdy")
        finish(kk)
        kk.run()
    return nc


def kernel(**inputs):
    raise NotImplementedError



FQ = [22, 22, 21, 21]
FQ_OFF = [0, 22, 44, 65]


def emit_ffn(kk, io, b):
    AB1, AB2, AB3 = b.AB1, b.AB2, b.AB3
    xn = AB1
    kk.dma("g", AB1[:, 0:32768], io.xn_sp.transpose([1, 0, 2]))
    kk.dma("g", AB3[:, 0:4096], io.wga.transpose([1, 0, 2]))
    kk.dma("g", AB3[:, 4096:8192], io.wua.transpose([1, 0, 2]))
    lald = kk.cnt("sd")
    kk.chain("v", [lambda e: e.memset(b.BC.ap(), 0.0),
                   lambda e: e.memset(b.LT.ap(), 0.0),
                   lambda e: e.memset(b.CST.ap(), 0.0)], inc="sv")
    kk.wait("v", "sv", kk.cnt("sv"))
    kk.wait("t", "sd", lald)

    for li, (wof, bi) in enumerate(((0, 4096), (512, 5))):
        wof = 0 if li == 0 else 4096
        bi = 4 + li
        for th in range(2):
            kk.use("t", f"bank{bi}")
            for kc in range(KC):
                kk.emit("t", lambda e, wof=wof, bi=bi, th=th, kc=kc: e.matmul(
                    out=b.pb[bi].ap(),
                    lhsT=AB3[:, wof + kc * 128:wof + (kc + 1) * 128],
                    rhs=xn[:, kc * 1024 + th * 512:kc * 1024 + (th + 1) * 512],
                    start=(kc == 0), stop=(kc == KC - 1)),
                    inc=None if kc < KC - 1 else "st")
            kk.wait("v", "st", kk.cnt("st"))
            kk.emit("v", lambda e, li=li, bi=bi, th=th: e.tensor_copy(
                out=b.BC[0:16, li * 1024 + th * 512:li * 1024 + (th + 1) * 512],
                in_=b.pb[bi][0:16, 0:512]), inc="sv")
            kk.guard(f"bank{bi}", "sv")
    xa_done = kk.cnt("sv")
    kk.guard("gring0", "st")
    kk.guard("gring1", "st")
    xag = b.BC[:, 0:1024]
    xau = b.BC[:, 1024:2048]
    kk.wait("t", "sv", xa_done)

    gring = [AB3[:, 0:4096], AB3[:, 4096:8192]]
    uring = [AB2[:, 22528:26624], AB2[:, 26624:30720]]
    hid = AB2
    ltb = b.LT.ap().bitcast(BF16)
    gsb = b.F1[:, 0:512].bitcast(BF16)

    for q in range(4):
        nq = FQ[q]
        hid_last = 0
        for hqi in range(nq):
            hc = FQ_OFF[q] + hqi
            par = hc % 2
            gsl, usl = gring[par], uring[par]
            kk.use("g", f"gring{par}")
            kk.dma("g", gsl, io.wg[hc].transpose([1, 0, 2]))
            kk.use("g", f"uring{par}")
            kk.dma("g", usl, io.wu[hc].transpose([1, 0, 2]))
            kk.use("g", f"lorab{par}")
            kk.dma("g", ltb[:, par * 128:par * 128 + 128], io.wgb[hc])
            kk.dma("g", ltb[:, 256 + par * 128:256 + par * 128 + 128],
                   io.wub[hc])
            kk.use("g", f"wda{par}")
            kk.dma("g", b.CST[:, par * 128:(par + 1) * 128], io.wda[hc])
            wld = kk.cnt("sd")
            kk.wait("t", "sd", wld)
            for (sl, bb, xa, lbo) in ((gsl, 0, xag, par * 128),
                                      (usl, 2, xau, 256 + par * 128)):
                for th in range(2):
                    bi = bb + th
                    kk.use("t", f"bank{bi}")
                    for kc in range(KC):
                        kk.emit("t", lambda e, sl=sl, bi=bi, th=th, kc=kc:
                                e.matmul(
                                    out=b.pb[bi].ap(),
                                    lhsT=sl[:, kc * 128:(kc + 1) * 128],
                                    rhs=xn[:, kc * 1024 + th * 512:
                                           kc * 1024 + (th + 1) * 512],
                                    start=(kc == 0), stop=False), inc=None)
                    kk.emit("t", lambda e, bi=bi, th=th, xa=xa, lbo=lbo:
                            e.matmul(out=b.pb[bi].ap(),
                                     lhsT=ltb[:, lbo:lbo + 128],
                                     rhs=xa[:, th * 512:(th + 1) * 512],
                                     start=False, stop=True), inc="st")
            kk.guard(f"gring{par}", "st")
            kk.guard(f"uring{par}", "st")
            kk.guard(f"lorab{par}", "st")
            mmd = kk.cnt("st")
            kk.wait("s", "st", mmd)
            kk.use("s", "gsb")
            for th in range(2):
                kk.emit("s", lambda e, th=th: e.activation(
                    out=gsb[:, th * 512:(th + 1) * 512],
                    in_=b.pb[th].ap(), func=AF.Silu), inc="ss")
                kk.guard(f"bank{th}", "ss")
            sld = kk.cnt("ss")
            kk.wait("v", "ss", sld)
            for th in range(2):
                kk.emit("v", lambda e, th=th, hqi=hqi: e.tensor_tensor(
                    out=hid[:, hqi * 1024 + th * 512:hqi * 1024 + (th + 1) * 512],
                    in0=b.pb[2 + th].ap(),
                    in1=gsb[:, th * 512:(th + 1) * 512], op=OP.mult), inc="sv")
                kk.guard(f"bank{2 + th}", "sv")
            kk.guard("gsb", "sv")
            hid_last = kk.cnt("sv")
            kk.wait("t", "sv", hid_last)
            for th in range(2):
                kk.emit("t", lambda e, th=th, hqi=hqi, hc=hc, par=par: e.matmul(
                    out=b.pb[6 + th].ap(),
                    lhsT=b.CST[:, par * 128:(par + 1) * 128],
                    rhs=hid[:, hqi * 1024 + th * 512:hqi * 1024 + (th + 1) * 512],
                    start=(hc == 0), stop=(hc == NFH - 1), skip_group_check=True),
                    inc=None if hc < NFH - 1 else "st")
            kk.guard(f"wda{par}", "sv")

        if q == 3:
            kk.wait("v", "st", kk.cnt("st"))
            for th in range(2):
                kk.emit("v", lambda e, th=th: e.tensor_copy(
                    out=b.BC[0:16, 2048 + th * 512:2048 + (th + 1) * 512],
                    in_=b.pb[6 + th][0:16, 0:512]), inc="sv")
                kk.guard(f"bank{6 + th}", "sv")
            kk.wait("t", "sv", kk.cnt("sv"))
        xad = b.BC[:, 2048:3072]
        dring = [AB3[:, 0:2816], AB3[:, 2816:5632]]
        kk.wait("t", "sv", hid_last)
        for dc in range(32):
            par = dc % 2
            dsl = dring[par][:, 0:nq * 128]
            kk.use("g", f"gring0")
            kk.use("g", f"gring1")
            kk.use("g", f"dring{par}")
            kk.dma("g", dsl, io.wd[dc, FQ_OFF[q]:FQ_OFF[q] + nq]
                   .transpose([1, 0, 2]))
            if q == 3:
                kk.use("g", f"lorab{par}")
                kk.dma("g", ltb[:, par * 128:(par + 1) * 128], io.wdb[dc])
            dld = kk.cnt("sd")
            kk.wait("t", "sd", dld)
            for th in range(2):
                bi = 4 + th
                kk.use("t", f"bank{bi}")
                for hqi in range(nq):
                    last = (hqi == nq - 1) and q < 3
                    kk.emit("t", lambda e, dsl=dsl, bi=bi, th=th, hqi=hqi,
                            last=last: e.matmul(
                                out=b.pb[bi].ap(),
                                lhsT=dsl[:, hqi * 128:(hqi + 1) * 128],
                                rhs=hid[:, hqi * 1024 + th * 512:
                                        hqi * 1024 + (th + 1) * 512],
                                start=(hqi == 0), stop=last),
                            inc="st" if last else None)
                if q == 3:
                    kk.emit("t", lambda e, bi=bi, th=th, par=par: e.matmul(
                        out=b.pb[bi].ap(),
                        lhsT=ltb[:, par * 128:(par + 1) * 128],
                        rhs=xad[:, th * 512:(th + 1) * 512],
                        start=False, stop=True), inc="st")
            kk.guard(f"dring{par}", "st")
            if q == 3:
                kk.guard(f"lorab{par}", "st")
            dmm = kk.cnt("st")
            kk.wait("v", "st", dmm)
            kk.use("v", f"dstg{par}")
            for th in range(2):
                kk.emit("v", lambda e, th=th, par=par: e.tensor_copy(
                    out=b.H[:, par * 2048 + th * 512:par * 2048 + (th + 1) * 512],
                    in_=b.pb[4 + th].ap()), inc="sv")
                kk.guard(f"bank{4 + th}", "sv")
            kk.wait("g", "sv", kk.cnt("sv"))
            dst = io.acc[dc].transpose([1, 0, 2])
            stg = b.H.ap()[:, par * 2048:par * 2048 + 1024] \
                .rearrange("p (tc f) -> p tc f", f=128)
            if q == 0:
                kk.dma("g", dst, stg, inc="sda")
            else:
                kk.dma("g", dst, stg, accum_op=OP.add, inc="sda")
            kk.guard(f"dstg{par}", "sda")
    kk.wait("y", "sda")


def build_L2(debug=()):
    nc = bass.Bass("TRN2", target_bir_lowering=False, debug=False)
    io = declare_dram(nc, "L2", debug=debug)
    kk = K(nc)
    with kk.stack:
        b = alloc_bufs(nc, kk.stack)
        load_consts(kk, io, b, "L2")
        emit_ffn(kk, io, b)
        finish(kk)
        kk.run()
    return nc



def emit_pass2(kk, io, b):
    AB1, AB2, AB3 = b.AB1, b.AB2, b.AB3
    idn = b.CST[:, 0:128]
    idnf = b.CSTF[:, 0:128]
    norw = b.CSTF
    kk.dma("g", b.CSTF[:, 256:256 + KC], io.norw)
    kk.dma("g", b.CST[:, 0:128], io.idn)
    kk.dma("g", b.BC[:, 0:1024], io.cfm_sp)
    kk.dma("g", b.SM1[0:64, 0:1024], io.al_sp)
    kk.dma("g", b.SM3[0:128, 0:128], io.alpha)
    kk.dma("g", b.SM3[0:8, 128:129], io.pmask)
    kk.chain("v", [lambda e: e.memset(b.SM4.ap(), 0.0)], inc="sv")
    kk.wait("v", "sv", kk.cnt("sv"))
    kk.wait("y", "sv", kk.cnt("sv"))
    kk.dma("y", b.SM4[0:8, 0:64],
           io.ag.rearrange("(k r) d -> k r d", r=129)[:, 128, 0:64], inc="sdy")
    ld0 = kk.cnt("sd")
    ld0y = kk.cnt("sdy")
    kk.wait("t", "sd", ld0)
    kk.wait("t", "sdy", ld0y)
    kk.use("t", "bank6")
    kk.emit("t", lambda e: e.matmul(out=b.pb[6][:, 0:64],
                                    lhsT=b.SM3[0:128, 0:128],
                                    rhs=b.SM4[:, 0:64], start=True, stop=True),
            inc="st")
    lmd = kk.cnt("st")
    kk.wait("s", "st", lmd)
    kk.emit("s", lambda e: e.activation(out=b.SM3[0:8, 129:193],
                                        in_=b.pb[6][0:8, 0:64], func=AF.Exp),
            inc="ss")
    kk.guard("bank6", "ss")
    kk.wait("v", "ss", kk.cnt("ss"))
    kk.emit("v", lambda e: e.tensor_scalar(
        out=b.SM3[0:8, 129:129 + 64], in0=b.SM3[0:8, 129:129 + 64],
        scalar1=b.SM3[0:8, 128:129], scalar2=None, op0=OP.mult), inc="sv")
    w8done = kk.cnt("sv")

    kk.chain("v", [lambda e: e.memset(b.H.ap(), 0.0)], inc="sv")
    kk.wait("v", "sv", kk.cnt("sv"))
    wrep = AB1.ap()[0:1, 0:8192].bitcast(F32)
    brhs = AB1.ap()[:, 0:8192].bitcast(F32)
    sseg = [b.F1[:, 0:512], b.F1[:, 512:1024]]
    for j in range(NCORES):
        kk.wait("y", "sv", w8done)
        kk.use("y", "wrow")
        kk.dma("y", b.SM3[0:1, 193:257],
               b.SM3[0:8, 129:193][j:j + 1, :], inc="sdy")
        kk.wait("g", "sdy", kk.cnt("sdy"))
        kk.use("g", "wrep")
        kk.emit("g", lambda e: e.tensor_copy(
            out=wrep.rearrange("o (r p2) -> o r p2", p2=64),
            in_=b.SM3[0:1, 193:257].unsqueeze(2).broadcast_to([1, 64, 64])),
            inc="sg")
        wrd = kk.cnt("sg")
        for seg in range(8):
            ssl = sseg[seg % 2]
            kk.use("y", f"sseg{seg % 2}")
            kk.dma("y", ssl, io.ag[129 * j:129 * j + 128,
                                   seg * 512:(seg + 1) * 512], inc="sdy")
            sld = kk.cnt("sdy")
            bi = seg % 2
            kk.use("t", f"bank{bi}")
            kk.wait("t", "sg", wrd)
            kk.emit("t", lambda e, bi=bi, seg=seg: e.matmul(
                out=b.pb[bi].ap(), lhsT=b.CSTF[:, 128:256],
                rhs=brhs[:, seg * 512:(seg + 1) * 512], start=True, stop=True),
                inc="st")
            wbd = kk.cnt("st")
            kk.wait("v", "st", wbd)
            kk.wait("v", "sdy", sld)
            kk.chain("v", [
                lambda e, ssl=ssl, bi=bi: e.tensor_tensor(
                    out=ssl, in0=ssl, in1=b.pb[bi].ap(), op=OP.mult),
                lambda e, ssl=ssl, seg=seg: e.tensor_tensor(
                    out=b.H[:, seg * 512:(seg + 1) * 512],
                    in0=b.H[:, seg * 512:(seg + 1) * 512], in1=ssl, op=OP.add),
            ], inc="sv")
            kk.guard(f"bank{bi}", "sv")
            kk.guard(f"sseg{seg % 2}", "sv")
        kk.guard("wrep", "st")
        kk.guard("wrow", "sg")
    hind = kk.cnt("sv")
    kk.wait("v", "sv", hind)
    Hbf = AB1[:, 8192:12288]
    kk.chain("v", [lambda e: e.tensor_copy(out=Hbf, in_=b.H.ap())], inc="sv")
    hbfd = kk.cnt("sv")

    ybuf = AB1[:, 16384:20480]
    zbuf = AB1[:, 20480:24576]
    scrib = AB1[:, 24576:28672]
    yn_tm = AB1[:, 28672:32768]
    wtm = b.LT[:, 0:64]
    yt = b.H
    u = b.F1[:, 0:512]
    SM = b.SM2

    for c in range(TC):
        kk.use("y", "ybuf")
        kk.dma("y", ybuf, io.y_sp[c], inc="sdy")
        kk.use("y", "zbuf")
        kk.dma("y", zbuf, io.z_sp[c], inc="sdy")
        yld = kk.cnt("sdy")
        kk.wait("s", "sv", None)
        kk.use("s", "wfm")
        kk.emit("s", lambda e, c=c: e.activation(
            out=b.F1[0:64, 1024:1152], in_=b.SM1[0:64, c * 128:(c + 1) * 128],
            func=AF.Exp), inc="ss")
        kk.wait("t", "ss", kk.cnt("ss"))
        kk.use("t", "bank7")
        kk.emit("t", lambda e: e.transpose(out=b.pb[7][:, 0:64],
                                           in_=b.F1[0:64, 1024:1152],
                                           identity=idnf[0:64, 0:64]), inc="st")
        kk.wait("v", "st", kk.cnt("st"))
        kk.use("v", "wtm")
        kk.emit("v", lambda e: e.tensor_copy(out=wtm, in_=b.pb[7][:, 0:64]),
                inc="sv")
        kk.guard("bank7", "sv")
        kk.guard("wfm", "st")
        wtmd = kk.cnt("sv")

        kk.wait("t", "sv", hbfd)
        for g8 in range(8):
            bi = 2 + g8 % 2
            kk.use("t", f"bank{bi}")
            kk.emit("t", lambda e, bi=bi, c=c, g8=g8: e.matmul(
                out=b.pb[bi].ap(), lhsT=b.BC[:, c * 128:(c + 1) * 128],
                rhs=Hbf[:, g8 * 512:(g8 + 1) * 512], start=True, stop=True),
                inc="st")
            yid = kk.cnt("st")
            kk.wait("v", "st", yid)
            kk.wait("v", "sv", wtmd)
            kk.wait("v", "sdy", yld)
            kk.chain("v", [
                lambda e, bi=bi, g8=g8: e.tensor_tensor(
                    out=u.rearrange("p (r hd) -> p r hd", hd=64),
                    in0=b.pb[bi].ap().rearrange("p (r hd) -> p r hd", hd=64),
                    in1=wtm[:, g8 * 8:(g8 + 1) * 8].unsqueeze(2)
                    .broadcast_to([128, 8, 64]), op=OP.mult),
                lambda e, g8=g8: e.tensor_tensor(
                    out=yt.ap()[:, g8 * 512:(g8 + 1) * 512],
                    in0=ybuf[:, g8 * 512:(g8 + 1) * 512], in1=u, op=OP.add),
                lambda e, g8=g8: e.tensor_tensor(
                    out=yt.ap()[:, g8 * 512:(g8 + 1) * 512],
                    in0=yt.ap()[:, g8 * 512:(g8 + 1) * 512],
                    in1=zbuf[:, g8 * 512:(g8 + 1) * 512], op=OP.mult),
            ], inc="sv")
            kk.guard(f"bank{bi}", "sv")
        gd = kk.cnt("sv")
        kk.guard("ybuf", "sv")
        kk.guard("zbuf", "sv")
        kk.guard("wtm", "sv")

        kk.wait("s", "sv", gd)
        kk.use("s", "scrib")
        kk.emit("s", lambda e: e.activation(out=scrib, in_=yt.ap(),
                                            func=AF.Square,
                                            accum_out=SM[:, 0:1]), inc="ss")
        kk.wait("v", "ss", kk.cnt("ss"))
        kk.chain("v", [
            lambda e: e.tensor_scalar(out=SM[:, 1:2], in0=SM[:, 0:1],
                                      scalar1=1.0 / D, scalar2=EPS,
                                      op0=OP.mult, op1=OP.add),
            lambda e: e.reciprocal(out=SM[:, 2:3], in_=SM[:, 1:2]),
        ], inc="sv")
        kk.wait("s", "sv", kk.cnt("sv"))
        kk.emit("s", lambda e: e.activation(out=SM[:, 3:4], in_=SM[:, 2:3],
                                            func=AF.Sqrt), inc="ss")
        kk.wait("v", "ss", kk.cnt("ss"))
        kk.use("v", "yn_tm")
        kk.emit("v", lambda e: e.tensor_scalar(out=yn_tm, in0=yt.ap(),
                                               scalar1=SM[:, 3:4], scalar2=None,
                                               op0=OP.mult), inc="sv")
        ynd = kk.cnt("sv")
        kk.guard("scrib", "sv")

        kk.wait("t", "sv", ynd)
        for dq in range(8):
            bi = 4 + dq % 2
            kk.use("t", f"bank{bi}")
            for j in range(4):
                dc = dq * 4 + j
                kk.emit("t", lambda e, dc=dc, j=j, bi=bi: e.transpose(
                    out=pbf(b.pb[bi])[:, j * 128:(j + 1) * 128],
                    in_=yn_tm[:, dc * 128:(dc + 1) * 128], identity=idn),
                    inc=None if j < 3 else "st")
            kk.wait("v", "st", kk.cnt("st"))
            kk.emit("v", lambda e, dq=dq, c=c, bi=bi: e.tensor_copy(
                out=AB2.ap()[:, 0:32768]
                .rearrange("p (dc t) -> p dc t", t=1024)
                [:, dq * 4:(dq + 1) * 4, c * 128:(c + 1) * 128],
                in_=pbf(b.pb[bi])[:, 0:512].rearrange("p (j t) -> p j t", t=128)),
                inc="sv")
            kk.guard(f"bank{bi}", "sv")
        kk.guard("yn_tm", "st")

    ynfd = kk.cnt("sv")
    kk.wait("v", "sv", ynfd)
    for dc in range(KC):
        kk.emit("v", lambda e, dc=dc: e.tensor_scalar(
            out=AB2.ap()[:, dc * 1024:(dc + 1) * 1024],
            in0=AB2.ap()[:, dc * 1024:(dc + 1) * 1024],
            scalar1=norw[:, 256 + dc:257 + dc], scalar2=None, op0=OP.mult),
            inc="sv")
    ynsc = kk.cnt("sv")
    kk.wait("t", "sv", ynsc)

    oring = [AB3[:, 0:4096], AB3[:, 4096:8192]]
    for oc in range(32):
        par = oc % 2
        osl = oring[par]
        kk.use("g", f"oring{par}")
        kk.dma("g", osl, io.wop[oc].transpose([1, 0, 2]))
        old = kk.cnt("sd")
        kk.wait("t", "sd", old)
        for th in range(2):
            bi = par * 2 + th
            kk.use("t", f"bank{bi}")
            for kc in range(KC):
                kk.emit("t", lambda e, osl=osl, bi=bi, th=th, kc=kc: e.matmul(
                    out=b.pb[bi].ap(), lhsT=osl[:, kc * 128:(kc + 1) * 128],
                    rhs=AB2.ap()[:, kc * 1024 + th * 512:
                                 kc * 1024 + (th + 1) * 512],
                    start=(kc == 0), stop=(kc == KC - 1)),
                    inc=None if kc < KC - 1 else "st")
        kk.guard(f"oring{par}", "st")
        omm = kk.cnt("st")
        kk.wait("v", "st", omm)
        kk.use("v", f"ostg{par}")
        for th in range(2):
            kk.emit("v", lambda e, th=th, par=par: e.tensor_copy(
                out=b.F1[:, 1024 + par * 1024 + th * 512:
                         1024 + par * 1024 + (th + 1) * 512]
                if False else
                b.F1[:, 1088 + par * 512:1088 + par * 512 + 512]
                if False else b.F1[:, 1536 + th * 512:1536 + (th + 1) * 512],
                in_=b.pb[par * 2 + th].ap()), inc="sv") if False else None
        for th in range(2):
            kk.emit("v", lambda e, th=th, par=par: e.tensor_copy(
                out=scrib.bitcast(F32)[:, par * 1024 + th * 512:
                                       par * 1024 + (th + 1) * 512]
                if False else
                AB1.ap()[:, 24576 + par * 4096:24576 + par * 4096 + 4096]
                .bitcast(F32)[:, th * 512:(th + 1) * 512],
                in_=b.pb[par * 2 + th].ap()), inc="sv")
            kk.guard(f"bank{par * 2 + th}", "sv")
        kk.wait("g", "sv", kk.cnt("sv"))
        stg = AB1.ap()[:, 24576 + par * 4096:24576 + par * 4096 + 4096] \
            .bitcast(F32)[:, 0:1024].rearrange("p (tc f) -> p tc f", f=128)
        kk.dma("g", io.acc[oc].transpose([1, 0, 2]), stg, accum_op=OP.add,
               inc="sda")
        kk.guard(f"ostg{par}", "sda")
    kk.wait("y", "sda")
    if io.acc2 is not io.acc:
        kk.wait("y", "sda")
        kk.dma("y", io.acc2, io.acc, inc="sdy")


def build_L3(debug=()):
    nc = bass.Bass("TRN2", target_bir_lowering=False, debug=False)
    io = declare_dram(nc, "L3", debug=debug)
    kk = K(nc)
    with kk.stack:
        b = alloc_bufs(nc, kk.stack)
        load_consts(kk, io, b, "L3")
        emit_pass2(kk, io, b)
        finish(kk)
        kk.run()
    return nc



def emit_final(kk, io, b):
    AB1, AB2 = b.AB1, b.AB2
    idnf = b.CSTF[:, 0:128]
    acc_src = io.acc2
    xr = [AB1.ap()[:, 0:8192].bitcast(F32), AB1.ap()[:, 8192:16384].bitcast(F32)]
    ar = [AB2.ap()[:, 0:8192].bitcast(F32), AB2.ap()[:, 8192:16384].bitcast(F32)]
    ost = b.H
    for tc in range(TC):
        par = tc % 2
        kk.use("y", f"xr{par}")
        kk.dma("y", xr[par],
               io.x.rearrange("(tc p) d -> tc p d", p=128)[tc], inc="sdy")
        kk.use("y", f"ar{par}")
        kk.dma("y", ar[par], acc_src[:, tc].transpose([1, 0, 2]), inc="sdy")
        ld = kk.cnt("sdy")
        kk.wait("t", "sdy", ld)
        kk.use("v", "ost")
        for dq in range(8):
            bi = 4 + dq % 2
            kk.use("t", f"bank{bi}")
            for j in range(4):
                dc = dq * 4 + j
                kk.emit("t", lambda e, dc=dc, j=j, bi=bi, par=par: e.transpose(
                    out=b.pb[bi][:, j * 128:(j + 1) * 128],
                    in_=ar[par].rearrange("p (dc f) -> p dc f", f=128)[:, dc, :]
                    if False else ar[par][:, dc * 128:(dc + 1) * 128],
                    identity=idnf), inc=None if j < 3 else "st")
            tdn = kk.cnt("st")
            kk.wait("v", "st", tdn)
            kk.emit("v", lambda e, dq=dq, bi=bi, par=par: e.scalar_tensor_tensor(
                out=ost.ap()[:, dq * 512:(dq + 1) * 512],
                in0=b.pb[bi].ap(), scalar=0.0,
                in1=xr[par][:, dq * 512:(dq + 1) * 512],
                op0=OP.add, op1=OP.add), inc="sv")
            kk.guard(f"bank{bi}", "sv")
        odone = kk.cnt("sv")
        kk.guard(f"xr{par}", "sv")
        kk.guard(f"ar{par}", "st")
        kk.wait("y", "sv", odone)
        kk.dma("y", io.out.rearrange("(tc p) d -> tc p d", p=128)[tc],
               ost.ap(), inc="sdo")
        kk.guard("ost", "sdo")
    kk.wait("y", "sdo")


def build_L4(debug=()):
    nc = bass.Bass("TRN2", target_bir_lowering=False, debug=False)
    io = declare_dram(nc, "L4", debug=debug)
    kk = K(nc)
    with kk.stack:
        b = alloc_bufs(nc, kk.stack)
        load_consts(kk, io, b, "L4")
        emit_final(kk, io, b)
        finish(kk)
        kk.run()
    return nc



K1KEYS = ("bc128", "x", "xh", "wip", "convw", "mup", "lnw", "alog", "dtb", "d_rep",
          "tri", "idn", "idnf", "ones1f", "epsc")
K2KEYS = ("bc128", "wg", "wu", "wd", "wga", "wua", "wgb", "wub", "wda", "wdb",
          "tri", "idn", "idnf", "ones1f", "epsc")
K3KEYS = ("bc128", "wop", "norw", "alpha", "pmask", "tri", "idn", "idnf", "ones1f",
          "epsc")
K4KEYS = ("bc128", "x", "tri", "idn", "idnf", "ones1f", "epsc")


def kernel(**inputs):
    per_core = relayout(inputs)
    nc1 = build_L1()
    r1 = run_bass_kernel_spmd(
        nc1, [{k: m[k] for k in K1KEYS} for m in per_core],
        core_ids=list(range(NCORES)))
    nc2 = build_L2()
    maps2 = []
    for c in range(NCORES):
        m = {k: per_core[c][k] for k in K2KEYS}
        m["xn_sp"] = np.asarray(r1.results[c]["xn_sp"])
        maps2.append(m)
    r2 = run_bass_kernel_spmd(nc2, maps2, core_ids=list(range(NCORES)))
    nc3 = build_L3()
    maps3 = []
    for c in range(NCORES):
        m = {k: per_core[c][k] for k in K3KEYS}
        for k in ("z_sp", "y_sp", "al_sp", "cfm_sp", "ag"):
            m[k] = np.asarray(r1.results[c][k])
        m["acc"] = np.asarray(r2.results[c]["acc"])
        maps3.append(m)
    r3 = run_bass_kernel_spmd(nc3, maps3, core_ids=list(range(NCORES)))
    nc4 = build_L4()
    maps4 = []
    for c in range(NCORES):
        m = {k: per_core[c][k] for k in K4KEYS}
        m["acc2"] = np.asarray(r3.results[c]["acc2"])
        maps4.append(m)
    r4 = run_bass_kernel_spmd(nc4, maps4, core_ids=list(range(NCORES)))
    out = np.concatenate([np.asarray(r4.results[c]["out"])[None]
                          for c in range(NCORES)], 0)
    return np.ascontiguousarray(out.reshape(2, 4096, D)).astype(np.float32)


def barrier(kk):
    for e in ENGINES:
        for sname in list(kk.sems.keys()):
            kk.wait(e, sname)


def build_fused():
    nc = bass.Bass("TRN2", target_bir_lowering=False, debug=False)
    io = declare_dram(nc, None)
    kk = K(nc)
    with kk.stack:
        b = alloc_bufs(nc, kk.stack)
        load_consts(kk, io, b, None)
        emit_ln(kk, io, b)
        emit_inproj(kk, io, b)
        dd = emit_dtpipe(kk, io, b)
        emit_pass1(kk, io, b, dd)
        barrier(kk)
        emit_ffn(kk, io, b)
        barrier(kk)
        emit_pass2(kk, io, b)
        barrier(kk)
        emit_final(kk, io, b)
        finish(kk)
        kk.run()
    return nc


KALL = tuple(sorted(set(K1KEYS + K2KEYS + K3KEYS + K4KEYS + ("norw",))))


def kernel(**inputs):
    per_core = relayout(inputs)
    nc = build_fused()
    res = run_bass_kernel_spmd(
        nc, [{k: m[k] for k in KALL} for m in per_core],
        core_ids=list(range(NCORES)))
    out = np.concatenate([np.asarray(res.results[c]["out"])[None]
                          for c in range(NCORES)], 0)
    return np.ascontiguousarray(out.reshape(2, 4096, D)).astype(np.float32)
